# version 02
# baseline (speedup 1.0000x reference)
import numpy as np
import ml_dtypes
import concourse.bass as bass
import concourse.mybir as mybir
from concourse.bass_utils import run_bass_kernel_spmd
from concourse.bass import IndirectOffsetOnAxis

F32 = mybir.dt.float32
BF16 = mybir.dt.bfloat16
I32 = mybir.dt.int32
AF = mybir.ActivationFunctionType
ALU = mybir.AluOpType
AX = mybir.AxisListType
bfd = ml_dtypes.bfloat16

D = 1024
L = 2048
NB = 2
NT = 4096
NTILE = 32
S = 16
PI = float(np.pi)
LOG1E8 = float(np.log(1e-8))
LOG1M = float(np.log1p(-1e-8))
INV2PI = float(1.0 / (2.0 * np.pi))
N2PI = float(-2.0 * np.pi)
ARW = 71808


class Sched:
    def __init__(self, race_mode=False):
        self.steps = []
        self.last_dma = {}
        self.race_mode = race_mode
        self.last_nondma = {}

    def add(self, eng, fn, deps=(), dma=False, q=None):
        deps = [d for d in deps if d is not None]
        sid = len(self.steps)
        for d in deps:
            assert 0 <= d < sid, (d, sid)
        key = (q or (eng + "d")) if dma else eng
        if dma:
            prev = self.last_dma.get(key)
            if prev is not None and prev not in deps:
                deps = deps + [prev]
            self.last_dma[key] = sid
        p = self.last_nondma.get(eng)
        if p is not None and p not in deps:
            deps = deps + [p]
        if not dma:
            self.last_nondma[eng] = sid
        self.steps.append(dict(eng=eng, fn=fn, deps=deps, dma=dma, key=key))
        return sid

    def _key(self, st):
        return st["key"]

    def sem_keys(self):
        return sorted({self._key(st) for st in self.steps})

    def emit(self, block, sems):
        needed = set()
        for st in self.steps:
            needed.update(st["deps"])
        cum = {}
        for i, st in enumerate(self.steps):
            k = self._key(st)
            if st["dma"] or i in needed:
                inc = 16 if st["dma"] else 1
                cum[k] = cum.get(k, 0) + inc
                st["inc"] = inc
                st["val"] = cum[k]
            else:
                st["inc"] = 0
                st["val"] = None
        final_vals = dict(cum)

        def run_engine(eng_obj, eng_name):
            if eng_name in ("s", "g"):
                eng_obj.wait_ge(sems["v"], 1)
            waited = {}
            for i, st in enumerate(self.steps):
                if st["eng"] != eng_name:
                    continue
                k_self = self._key(st)
                for d in st["deps"]:
                    dst = self.steps[d]
                    dk = self._key(dst)
                    v = dst["val"]
                    if waited.get(dk, -1) < v:
                        eng_obj.wait_ge(sems[dk], v)
                        waited[dk] = v
                ins = st["fn"](eng_obj)
                if st["inc"]:
                    assert ins is not None, f"step {i} ({eng_name}) returned None"
                    ins.then_inc(sems[k_self], st["inc"])

        block.tensor(lambda e: run_engine(e, "t"))
        block.vector(lambda e: run_engine(e, "v"))
        block.scalar(lambda e: run_engine(e, "s"))
        block.gpsimd(lambda e: run_engine(e, "g"))

        def _sync(y):
            run_engine(y, "y")
            for k, v in final_vals.items():
                y.wait_ge(sems[k], v)
        block.sync(_sync)


def _consts():
    c = {}
    c["id_bf"] = np.eye(128, dtype=bfd)
    c["id_f32"] = np.eye(128, dtype=np.float32)
    c["tri_f32"] = (np.arange(128)[:, None] <= np.arange(128)[None, :]).astype(np.float32)
    c["ones_128x1_bf"] = np.ones((128, 1), bfd)
    c["ones_1x128_bf"] = np.ones((1, 128), bfd)
    e0 = np.zeros((128, 145), np.float32)
    e0[:, 0:128] = np.arange(128, dtype=np.float32)[None, :]
    for j in range(129, 145):
        e0[:, j] = 128.0 * (j - 129)
    c["e0"] = e0
    io = (np.arange(16)[None, :] * 128 + np.arange(128)[:, None]).astype(np.int32)
    c["iota_col"] = io
    c["iota_col2"] = io + 2048
    inv = (10000.0 ** (-(np.arange(128) % 32) * 2.0 / 64.0)).astype(np.float32)
    inv = np.where((np.arange(128) % 64) < 32,
                   (10000.0 ** (-(np.arange(128) % 64) * 2.0 / 64.0)).astype(np.float32),
                   (10000.0 ** (-((np.arange(128) % 64) - 32) * 2.0 / 64.0)).astype(np.float32))
    c["inv2"] = np.broadcast_to(inv[None, :], (128, 128)).copy().astype(np.float32)
    R = np.zeros((128, 128), bfd)
    for B in (0, 64):
        for j in range(32):
            R[B + 32 + j, B + j] = -1.0
            R[B + j, B + 32 + j] = 1.0
    c["rot"] = R
    for qc in (0, 1):
        r = np.arange(128)[:, None] + 128 * qc
        cc = np.arange(256)[None, :]
        t = (cc <= r).astype(bfd)
        t[:, 255] = 0
        c[f"tri{qc}"] = t
    c["q8"] = np.full((128, 1), 0.125, np.float32)
    c["cjnk0"] = np.broadcast_to((257.0 + np.arange(16, dtype=np.float32))[None, :],
                                 (128, 16)).copy()
    c["cjnk1"] = np.broadcast_to((17.0 + np.arange(16, dtype=np.float32))[None, :],
                                 (128, 16)).copy()
    c["zero_i32"] = np.zeros((1, 1), np.int32)
    return c


DEBUG_DUMPS = []


def build_program(debug_names=(), detect_races=False):
    nc = bass.Bass("TRN2", target_bir_lowering=False,
                   detect_race_conditions=detect_races)

    x_h = nc.dram_tensor("x", [NT, D], F32, kind="ExternalInput")
    wint_h = nc.dram_tensor("wint", [D, 2 * D], F32, kind="ExternalInput")
    woutt_h = nc.dram_tensor("woutt", [D, D], F32, kind="ExternalInput")
    wbt_h = nc.dram_tensor("wbt", [D, S], F32, kind="ExternalInput")
    wct_h = nc.dram_tensor("wct", [S, D], F32, kind="ExternalInput")
    wq_h = nc.dram_tensor("wq", [D, D], F32, kind="ExternalInput")
    wk_h = nc.dram_tensor("wk", [D, D], F32, kind="ExternalInput")
    wv_h = nc.dram_tensor("wv", [D, D], F32, kind="ExternalInput")
    wo_h = nc.dram_tensor("wo", [D, D], F32, kind="ExternalInput")
    w1_h = nc.dram_tensor("w1", [D, 4 * D], F32, kind="ExternalInput")
    w2_h = nc.dram_tensor("w2", [4 * D, D], F32, kind="ExternalInput")
    rw_h = nc.dram_tensor("rw", [D], F32, kind="ExternalInput")
    rb_h = nc.dram_tensor("rb", [1], F32, kind="ExternalInput")
    lng_h = nc.dram_tensor("lng", [D], F32, kind="ExternalInput")
    lnb_h = nc.dram_tensor("lnb", [D], F32, kind="ExternalInput")
    bin_h = nc.dram_tensor("binv", [2 * D], F32, kind="ExternalInput")
    bout_h = nc.dram_tensor("boutv", [D], F32, kind="ExternalInput")
    dskip_h = nc.dram_tensor("dskip", [D], F32, kind="ExternalInput")
    logdt_h = nc.dram_tensor("logdt", [D], F32, kind="ExternalInput")
    alog_h = nc.dram_tensor("alog", [D, S], F32, kind="ExternalInput")
    ln1_h = nc.dram_tensor("ln1", [D], F32, kind="ExternalInput")
    ln2_h = nc.dram_tensor("ln2", [D], F32, kind="ExternalInput")
    pos_h = nc.dram_tensor("posr", [NB, L], I32, kind="ExternalInput")
    out_h = nc.dram_tensor("out", [NT, D], F32, kind="ExternalOutput")

    CN = _consts()
    c_h = {k: nc.inline_tensor(v, name="c_" + k) for k, v in CN.items()}

    apad_h = nc.dram_tensor("d_apad", [S, 2176], BF16, kind="Internal")
    sg_h = nc.dram_tensor("d_sg", [NB * L, 4], F32, kind="Internal")
    possel_h = nc.dram_tensor("d_possel", [1024], I32, kind="Internal")
    rowa_h = nc.dram_tensor("d_rowa", [8, 4096], F32, kind="Internal")

    from contextlib import ExitStack
    ctx = ExitStack()

    def sbt(name, shape, dtype):
        return ctx.enter_context(nc.sbuf_tensor(name, shape, dtype))

    def pst(name, shape, dtype=F32):
        return ctx.enter_context(nc.psum_tensor(name, shape, dtype))

    cid_bf = sbt("cid_bf", [128, 128], BF16)
    cid_f = sbt("cid_f", [128, 128], F32)
    ctri_f = sbt("ctri_f", [128, 128], F32)
    cones_cb = sbt("cones_cb", [128, 1], BF16)
    cones_rb = sbt("cones_rb", [1, 128], BF16)
    ce0 = sbt("ce0", [128, 145], F32)
    ciota = sbt("ciota", [128, 16], I32)
    ciota2 = sbt("ciota2", [128, 16], I32)
    cinv2 = sbt("cinv2", [128, 128], F32)
    crot = sbt("crot", [128, 128], BF16)
    ctri0 = sbt("ctri0", [128, 256], BF16)
    ctri1 = sbt("ctri1", [128, 256], BF16)
    cq8 = sbt("cq8", [128, 1], F32)
    cjk0 = sbt("cjk0", [128, 16], F32)
    cjk1 = sbt("cjk1", [128, 16], F32)
    czero_i = sbt("czero_i", [1, 1], I32)

    ONES_R = lambda n=128: ctri_f[0:1, 0:n]
    ONES_C = lambda: ctri_f[:, 127:128]

    v_lng = sbt("v_lng", [128, 8], F32)
    v_lnbb = sbt("v_lnbb", [128, 8], BF16)
    v_bin = sbt("v_bin", [128, 16], F32)
    v_dsk = sbt("v_dsk", [128, 8], F32)
    v_logdt = sbt("v_logdt", [128, 8], F32)
    v_ln1 = sbt("v_ln1", [128, 8], F32)
    v_ln2 = sbt("v_ln2", [128, 8], F32)
    v_rw = sbt("v_rw", [128, 8], F32)
    v_rb = sbt("v_rb", [1, 1], F32)
    v_la = sbt("v_la", [128, 128], F32)
    v_dt = sbt("v_dt", [128, 8], F32)
    v_ndt = sbt("v_ndt", [128, 8], F32)
    v_dtm16 = sbt("v_dtm16", [16, 1], F32)
    v_sm = sbt("v_sm", [128, 8], F32)
    v_bb = sbt("v_bb", [128, 16], F32)
    wbt_sb = sbt("wbt_sb", [128, 128], BF16)
    wct_sb = sbt("wct_sb", [16, 1024], BF16)
    bout_rb = sbt("bout_rb", [1, 1024], BF16)
    lrow = sbt("lrow", [1, 512], F32)

    st_sumx = sbt("st_sumx", [128, 32], F32)
    st_sumq = sbt("st_sumq", [128, 32], F32)
    st_mu0 = sbt("st_mu0", [128, 32], F32)
    st_r0 = sbt("st_r0", [128, 32], F32)
    st_alp = sbt("st_alp", [128, 32], F32)
    st_nma = sbt("st_nma", [128, 32], F32)
    st_wrk = sbt("st_wrk", [128, 64], F32)
    dbg_sb = sbt("dbg_sb", [128, 320], F32)
    dbg_probe_sts = []
    vdum = sbt("vdum", [128, 8], F32)

    _spc = [0]

    def SPB(e, ins):
        _spc[0] += 1
        ins.then_inc(sems["sp"], 1)
        e.wait_ge(sems["sp"], _spc[0])

    def CH(e, *fns):
        ins = None
        for f in fns:
            if ins is not None:
                SPB(e, ins)
            ins = f(e)
        return ins

    rk_w = sbt("rk_w", [128, 32], F32)
    rk_rank = sbt("rk_rank", [128, 32], F32)
    rk_sel = sbt("rk_sel", [128, 32], F32)
    rk_csum = sbt("rk_csum", [128, 32], F32)
    rk_omf = sbt("rk_omf", [128, 32], F32)
    rk_omi = sbt("rk_omi", [128, 32], I32)
    rk_sg = sbt("rk_sg", [128, 64], F32)
    rk_b1 = sbt("rk_b1", [128, 32], F32)
    rk_b2 = sbt("rk_b2", [128, 32], F32)

    sp_idx = sbt("sp_idx", [128, 4], I32)
    sp_posi = sbt("sp_posi", [128, 4], I32)
    sp_posf = sbt("sp_posf", [128, 4], F32)
    sp_sg = sbt("sp_sg", [128, 16], F32)
    sp_r1 = sbt("sp_r1", [128, 16], F32)

    WAR = sbt("WAR", [128, 16384], BF16)
    BT = sbt("BT", [128, 256], BF16)
    XMS = sbt("XMS", [128, 256], BF16)
    XMT = sbt("XMT", [16, 2048], BF16)
    AR = sbt("AR", [128, ARW], BF16)

    PS0 = pst("PS0", [128, 512], F32)
    PS1 = pst("PS1", [128, 512], F32)
    PS2 = pst("PS2", [128, 512], F32)
    PS3 = pst("PS3", [128, 512], F32)
    PS4 = pst("PS4", [128, 1024], BF16)
    PS5 = pst("PS5", [128, 1024], F32)
    PS6 = pst("PS6", [17, 512], F32)
    PS7 = PS6

    SEM_KEYS = ["t", "v", "s", "g", "gd", "yd", "sd", "gw", "gc", "ga", "gh",
                "gr", "gg", "gp", "yo", "gs0", "gs1", "gs2", "gs3", "sp"]
    sems = {k: ctx.enter_context(nc.semaphore("sem_" + k)) for k in SEM_KEYS}
    sch = Sched(race_mode=detect_races)
    art = AR[:, :].tensor
    wart = WAR[:, :].tensor

    def arb(col0, parts, cols):
        return bass.AP(art, col0, [[ARW, parts], [1, cols]])

    def arb2(col0, parts, dims):
        return bass.AP(art, col0, [[ARW, parts]] + dims)

    def arf(col0, parts, elems):
        return bass.AP(art, col0, [[ARW, parts], [1, 2 * elems]]).bitcast(F32)

    def arf2(col0, parts, dims):
        return bass.AP(art, col0, [[ARW, parts]] + dims).bitcast(F32)

    dump_handles = {}

    def dump(name, ap_fn, shape, dtype, deps):
        if name not in debug_names:
            return None
        h = nc.dram_tensor("dbg_" + name, shape, dtype, kind="ExternalOutput")
        dump_handles[name] = h
        return sch.add("g", lambda e: e.dma_start(h.ap(), ap_fn()), deps=deps,
                       dma=True, q="gc")

    P_XP = 0
    P_ZG = 16384
    A_XT = [32768, 34816]
    A_XFM = [36864, 38912]
    A_UT = [40960, 41984]
    A_UFM = 43008
    A_JUNK = 48128
    A_BREV = 49152
    R_WBC = 32768
    R_GB = [36864, 38912]
    C_HREV = [58496, 60544]
    B_XSEL = 36864
    B_PROC = 40960
    B_HN = 45056
    B_HNF = 47104
    B_QF = 49152
    B_KF = 51200
    B_VF = 53248
    B_COS = 55296
    B_SIN = 55552
    B_EB = 55808
    B_AT = 56320
    B_VT = 56832
    B_H1 = 56960
    B_SCR = 57472
    B_ANG = 57984
    C_OUTT = [62592, 64640]
    C_GATED = 66688
    C_XRE = [67712, 69760]
    TBL_X = 32768
    TBL_ES = 35104

    def gld(dst, src, deps=(), q="gc"):
        return sch.add("g", lambda e, d=dst, s=src: e.dma_start(d, s), deps=deps,
                       dma=True, q=q)

    lc = {}
    lc["id_bf"] = gld(cid_bf[:], c_h["id_bf"].ap())
    lc["id_f"] = gld(cid_f[:], c_h["id_f32"].ap())
    lc["tri_f"] = gld(ctri_f[:], c_h["tri_f32"].ap())
    lc["ones_cb"] = gld(cones_cb[:], c_h["ones_128x1_bf"].ap())
    lc["ones_rb"] = gld(cones_rb[:], c_h["ones_1x128_bf"].ap())
    lc["e0"] = gld(ce0[:], c_h["e0"].ap())
    lc["iota"] = gld(ciota[:], c_h["iota_col"].ap())
    lc["iota2"] = gld(ciota2[:], c_h["iota_col2"].ap())
    lc["inv2"] = gld(cinv2[:], c_h["inv2"].ap())
    lc["rot"] = gld(crot[:], c_h["rot"].ap())
    lc["tri0"] = gld(ctri0[:], c_h["tri0"].ap())
    lc["tri1"] = gld(ctri1[:], c_h["tri1"].ap())
    lc["q8"] = gld(cq8[:], c_h["q8"].ap())
    lc["cjk0"] = gld(cjk0[:], c_h["cjnk0"].ap())
    lc["cjk1"] = gld(cjk1[:], c_h["cjnk1"].ap())
    lc["zi"] = gld(czero_i[:], c_h["zero_i32"].ap())

    def fmcols(h, nch):
        return bass.AP(h, 0, [[1, 128], [128, nch]])

    ld_lng = gld(v_lng[:], fmcols(lng_h, 8))
    ld_lnbb = gld(v_lnbb[:], fmcols(lnb_h, 8))
    ld_bin = gld(v_bin[:], fmcols(bin_h, 16))
    ld_dsk = gld(v_dsk[:], fmcols(dskip_h, 8))
    ld_logdt = gld(v_logdt[:], fmcols(logdt_h, 8))
    ld_ln1 = gld(v_ln1[:], fmcols(ln1_h, 8))
    ld_ln2 = gld(v_ln2[:], fmcols(ln2_h, 8))
    ld_rw = gld(v_rw[:], fmcols(rw_h, 8))
    ld_rb = gld(v_rb[:], bass.AP(rb_h, 0, [[1, 1], [1, 1]]))
    ld_la = gld(v_la[:], bass.AP(alog_h, 0, [[16, 128], [2048, 8], [1, 16]]))
    ld_wbt = gld(wbt_sb[:], bass.AP(wbt_h, 0, [[16, 128], [2048, 8], [1, 16]]))
    ld_wct = gld(wct_sb[:], wct_h.ap())
    ld_bout = gld(bout_rb[:], bass.AP(bout_h, 0, [[1024, 1], [1, 1024]]))
    ld_wint = gld(bass.AP(wart, 0, [[16384, 128], [2048, 8], [1, 2048]]),
                  bass.AP(wint_h, 0, [[2048, 128], [128 * 2048, 8], [1, 2048]]), q="gw")

    def z0f(e):
        e.memset(vdum[:, :], 0.0)
        return e.memset(arf(A_JUNK, 128, 128), 0.0)
    v_z0 = sch.add("v", z0f)
    z_sg = sch.add("g", lambda e: e.dma_start(
        bass.AP(sg_h, 0, [[128, 128], [1, 128]]), arf(A_JUNK, 128, 128)),
        deps=[v_z0], dma=True, q="gr")
    v_z1 = sch.add("v", lambda e: e.memset(lrow[:], 0.0), deps=[v_z0])
    def zmisc(e):
        e.memset(sp_posi[:], 0)
        e.memset(sp_posf[:], 0.0)
        e.memset(sp_sg[:], 0.0)
        e.memset(sp_r1[:], 0.0)
        return e.memset(sp_idx[:], 0)
    z_misc = sch.add("v", zmisc, deps=[v_z0])
    z_ps = sch.add("g", lambda e: e.dma_start(
        bass.AP(possel_h, 0, [[1, 512], [1, 1]]), lrow[:]),
        deps=[v_z1], dma=True, q="gr")

    s_exdt = sch.add("s", lambda e: e.activation(v_dt[:], v_logdt[:], AF.Exp), deps=[ld_logdt])
    v_cldt = sch.add("v", lambda e: e.tensor_scalar(v_dt[:], v_dt[:], 1e-4, 1.0,
                                                    op0=ALU.max, op1=ALU.min), deps=[s_exdt])
    v_negdt = sch.add("v", lambda e: e.tensor_scalar(v_ndt[:], v_dt[:], -1.0, None,
                                                     op0=ALU.mult), deps=[v_cldt])

    def mm_dtm(e):
        last = None
        for c in range(8):
            last = e.matmul(PS7[0:1, 0:1], v_dt[:, c:c + 1], ONES_C(),
                            start=(c == 0), stop=(c == 7))
        return last
    t_dtm = sch.add("t", mm_dtm, deps=[v_cldt, lc["tri_f"]])
    v_dtmv = sch.add("v", lambda e: e.tensor_scalar(v_sm[0:1, 0:1], PS7[0:1, 0:1],
                                                    1.0 / 1024.0, None, op0=ALU.mult),
                     deps=[t_dtm])
    t_dtmb = sch.add("t", lambda e: e.matmul(PS7[0:16, 8:9], ONES_R(16), v_sm[0:1, 0:1],
                                             start=True, stop=True), deps=[v_dtmv])
    v_dtmb = sch.add("v", lambda e: e.tensor_copy(v_dtm16[:], PS7[0:16, 8:9]), deps=[t_dtmb])

    def mm_rws(e):
        last = None
        for c in range(8):
            last = e.matmul(PS7[0:1, 16:17], v_rw[:, c:c + 1], ONES_C(),
                            start=(c == 0), stop=(c == 7))
        return last
    t_rws = sch.add("t", mm_rws, deps=[ld_rw, lc["tri_f"], v_dtmb])
    v_rws = sch.add("v", lambda e: e.tensor_copy(v_sm[0:1, 1:2], PS7[0:1, 16:17]), deps=[t_rws])
    t_rwsb = sch.add("t", lambda e: e.matmul(PS2[:, 0:1], ONES_R(), v_sm[0:1, 1:2],
                                             start=True, stop=True), deps=[v_rws])
    v_rwsb = sch.add("v", lambda e: e.tensor_copy(v_sm[:, 2:3], PS2[:, 0:1]), deps=[t_rwsb])
    t_rbb = sch.add("t", lambda e: e.matmul(PS2[:, 1:2], ONES_R(), v_rb[:],
                                            start=True, stop=True), deps=[ld_rb, v_rwsb])
    v_rbb = sch.add("v", lambda e: e.tensor_copy(v_sm[:, 3:4], PS2[:, 1:2]), deps=[t_rbb])

    s_exal = sch.add("s", lambda e: e.activation(v_la[:], v_la[:], AF.Exp), deps=[ld_la])
    la_steps = []
    for c in range(8):
        la_steps.append(sch.add("v", lambda e, c=c: e.tensor_scalar(
            v_la[:, c * 16:(c + 1) * 16], v_la[:, c * 16:(c + 1) * 16],
            v_ndt[:, c:c + 1], None, op0=ALU.mult), deps=[s_exal, v_negdt]))
    v_clla = sch.add("v", lambda e: e.tensor_scalar(v_la[:], v_la[:], LOG1E8, LOG1M,
                                                    op0=ALU.max, op1=ALU.min), deps=la_steps)

    apad_done = []
    prev_tbl = None
    for s_i in range(S):
        xs = arf2(TBL_X, 128, [[292, 8], [1, 290]])
        in0 = bass.AP(ce0[:, :].tensor, 0, [[145, 128], [0, 8], [1, 145]])
        in1 = bass.AP(v_la[:, :].tensor, s_i, [[128, 128], [16, 8], [0, 145]])
        v_x = sch.add("v", lambda e, xs=xs, in0=in0, in1=in1:
                      e.tensor_tensor(xs, in0, in1, ALU.mult),
                      deps=[v_clla] + ([prev_tbl] if prev_tbl else []))
        esb = arb2(TBL_ES, 128, [[146, 8], [1, 145]])
        s_e = sch.add("s", lambda e, esb=esb, xs=xs: e.activation(esb, xs, AF.Exp), deps=[v_x])
        zc = arb2(TBL_ES + 128, 128, [[146, 8], [1, 1]])
        v_z = sch.add("v", lambda e, zc=zc: e.memset(zc, 0.0), deps=[s_e])

        def es_mm(e):
            last = None
            for c in range(8):
                lhs = arb(TBL_ES + c * 146 + 128, 128, 17)
                rhs = arb(TBL_ES + c * 146, 128, 128)
                last = e.matmul(PS6[0:17, 0:128], lhs, rhs, start=(c == 0), stop=(c == 7))
            return last
        t_mm = sch.add("t", es_mm, deps=[v_z, v_rws, v_dtmb])
        v_ev = sch.add("v", lambda e: e.tensor_scalar(
            arb(TBL_X, 17, 128), PS6[0:17, 0:128], 1.0 / 1024.0, None, op0=ALU.mult),
            deps=[t_mm])
        d_ap = sch.add("g", lambda e, s_i=s_i: e.dma_start(
            bass.AP(apad_h, s_i * 2176, [[128, 17], [1, 128]]), arb(TBL_X, 17, 128)),
            deps=[v_ev], dma=True, q="ga")
        apad_done.append(d_ap)
        prev_tbl = d_ap

    dump("apad", lambda: bass.AP(apad_h, 0, [[2176, 16], [1, 2176]]),
         [16, 2176], F32, [apad_done[-1]])

    logrow_last = None
    bb_sts = []
    for h in range(2):
        def mm_bw(e, h=h):
            last = None
            for c in range(8):
                for h2 in range(2):
                    rhs = bass.AP(wart, c * 2048 + h * 1024 + h2 * 512,
                                  [[16384, 128], [1, 512]])
                    last = e.matmul(PS5[0:1, h2 * 512:(h2 + 1) * 512], v_lnbb[:, c:c + 1],
                                    rhs, start=(c == 0), stop=(c == 7))
            return last
        t_bw = sch.add("t", mm_bw, deps=[ld_wint, ld_lnbb] + bb_sts)
        for piece in range(2):
            v_e = sch.add("v", lambda e, piece=piece: e.tensor_copy(
                lrow[:], PS5[0:1, piece * 512:(piece + 1) * 512]),
                deps=[t_bw, logrow_last, z_ps])
            d_s = sch.add("g", lambda e, h=h, piece=piece: e.dma_start(
                bass.AP(rowa_h, 6 * 4096 + h * 1024 + piece * 512, [[1, 1], [1, 512]]),
                lrow[:]), deps=[v_e], dma=True, q="gr")
            logrow_last = d_s
            bb_sts.append(d_s)
    ld_bwin = sch.add("g", lambda e: e.dma_start(
        v_bb[:], bass.AP(rowa_h, 6 * 4096, [[1, 128], [128, 16]])), deps=bb_sts,
        dma=True, q="gr")
    v_bbadd = sch.add("v", lambda e: e.tensor_add(v_bb[:], v_bb[:], v_bin[:]),
                      deps=[ld_bwin, ld_bin])
    wg_steps = []
    for c in range(8):
        def f(e, c=c):
            w = bass.AP(wart, c * 2048, [[16384, 128], [1, 2048]])
            e.tensor_scalar(w, w, v_lng[:, c:c + 1], None, op0=ALU.mult)
            return e.tensor_scalar(vdum[:, 0:2], vdum[:, 2:4], 1.0, None, op0=ALU.mult)
        wg_steps.append(sch.add("v", f, deps=[ld_wint, ld_lng] + bb_sts))

    ps01_last = {0: None, 1: None}
    ps5_last = bb_sts[-1]
    ps4_last = None
    rt_slot_ev = {}
    prev_b = {}
    scatter_all = []
    for b in range(NB):
        xt_dma = [None] * 16
        u_done = [None] * 16
        ufm_ev = [None] * 16
        rt_mm = [None] * 16
        rt_ev = [None] * 16
        stat_done = [None] * 16
        sumq_s = [None] * 16
        xfm_ev = [None] * 16
        xz_ev = []
        brev_ev = []
        dsc_steps = []
        sumx_s = [None] * 16
        t_x_s = [None] * 16
        last_winmm = None
        if b:
            ld_wint_b = gld(bass.AP(wart, 0, [[16384, 128], [2048, 8], [1, 2048]]),
                            bass.AP(wint_h, 0, [[2048, 128], [128 * 2048, 8], [1, 2048]]),
                            deps=prev_b["war_free"] + prev_b["gated_all"][-1:], q="gw")
            wg_b = []
            for c in range(8):
                def fgs(e, c=c):
                    w = bass.AP(wart, c * 2048, [[16384, 128], [1, 2048]])
                    e.tensor_scalar(w, w, v_lng[:, c:c + 1], None, op0=ALU.mult)
                    return e.tensor_scalar(vdum[:, 0:2], vdum[:, 2:4], 1.0, None, op0=ALU.mult)
                wg_b.append(sch.add("v", fgs, deps=[ld_wint_b, ld_lng]))
            wg_cur = wg_b
        else:
            wg_cur = wg_steps

        def emit_tail(tc):
            nonlocal ps4_last, ps01_last, last_winmm
            tg = b * 16 + tc
            xbuf = A_XT[tc % 2]
            xt_ap = arf(xbuf, 128, 1024)
            ubuf = A_UT[tc % 2]
            u_ap = arb(ubuf, 128, 1024)
            dep_u = [stat_done[tc], xt_dma[tc]]
            if tc >= 2:
                dep_u.append(ufm_ev[tc - 2])
            if b and tc < 2:
                dep_u += prev_b["proc_free"]
            u_done[tc] = sch.add("s", lambda e, u_ap=u_ap, tg=tg, xt_ap=xt_ap: e.activation(
                u_ap, xt_ap, AF.Identity, scale=st_alp[:, tg:tg + 1],
                bias=st_nma[:, tg:tg + 1]), deps=dep_u)
            j = tc % 4
            def utp(e, ubuf=ubuf):
                last = None
                for c in range(8):
                    srcp = bass.AP(art, ubuf + 128 * c, [[ARW, 128], [1, 128]])
                    last = e.transpose(PS4[:, c * 128:(c + 1) * 128], srcp, cid_bf[:])
                return last
            t_u = sch.add("t", utp, deps=[u_done[tc], lc["id_bf"], ps4_last])
            def uev(e, j=j):
                dst = arb2(A_UFM + j * 128, 128, [[512, 8], [1, 128]])
                return e.tensor_copy(dst, bass.AP(PS4[:, :].tensor, 0, [[1024, 128], [1, 1024]]))
            dep_ue = [t_u]
            if last_winmm is not None and j == 0:
                dep_ue.append(last_winmm)
            ufm_ev[tc] = sch.add("v", uev, deps=dep_ue)
            ps4_last = ufm_ev[tc]

            if j == 3:
                kk = tc // 4
                blk = [ufm_ev[tc - i] for i in range(4)] + wg_cur
                if b:
                    blk += prev_b["gated_all"]
                for m in range(16):
                    bank, bi = (PS0, 0) if m % 2 == 0 else (PS1, 1)
                    def wmm(e, m=m, bank=bank):
                        last = None
                        for c in range(8):
                            lhs = bass.AP(wart, c * 2048 + m * 128, [[16384, 128], [1, 128]])
                            rhs = arb(A_UFM + c * 512, 128, 512)
                            last = e.matmul(bank[:, :], lhs, rhs, start=(c == 0), stop=(c == 7))
                        return last
                    t_mm = sch.add("t", wmm, deps=blk + [ps01_last[bi]])
                    last_winmm = t_mm
                    if m < 8:
                        def ev(e, m=m, kk=kk, bank=bank):
                            dst = bass.AP(art, P_XP + m * 2048 + kk * 512,
                                          [[ARW, 128], [1, 512]])
                            return e.tensor_scalar(dst, bank[:, :], v_bb[:, m:m + 1], None,
                                                   op0=ALU.add)
                        e_mm = sch.add("v", ev, deps=[t_mm, v_bbadd])
                    else:
                        def ev(e, m=m, kk=kk, bank=bank):
                            dst = bass.AP(art, P_ZG + (m - 8) * 2048 + kk * 512,
                                          [[ARW, 128], [1, 512]])
                            return e.activation(dst, bank[:, :], AF.Silu, bias=v_bb[:, m:m + 1])
                        e_mm = sch.add("s", ev, deps=[t_mm, v_bbadd])
                    xz_ev.append(e_mm)
                    ps01_last[bi] = e_mm
                def wbmm(e, kk=kk):
                    last = None
                    for c in range(8):
                        rhs = bass.AP(art, P_XP + c * 2048 + kk * 512, [[ARW, 128], [1, 512]])
                        last = e.matmul(PS7[0:16, 0:512], wbt_sb[:, c * 16:(c + 1) * 16], rhs,
                                        start=(c == 0), stop=(c == 7))
                    return last
                dep_wb = [xz_ev[-16 + i] for i in range(8)] + [ld_wbt, v_dtmb,
                                                              apad_done[-1]]
                if brev_ev:
                    dep_wb.append(brev_ev[-1])
                elif b:
                    dep_wb.append(prev_b["brev_last"])
                t_wb = sch.add("t", wbmm, deps=dep_wb)
                tk0 = kk * 512
                def bev(e, tk0=tk0):
                    dst = bass.AP(art, A_BREV + (2047 - tk0), [[ARW, 16], [-1, 512]])
                    e.tensor_scalar(dst, PS7[0:16, 0:512], v_dtm16[:], None, op0=ALU.mult)
                    return e.tensor_scalar(vdum[:, 0:2], vdum[:, 2:4], 1.0, None, op0=ALU.mult)
                brev_ev.append(sch.add("v", bev, deps=[t_wb] +
                               ([prev_b["bt_done"]] if (b and kk == 0) else [])))
                def dsc(e, kk=kk):
                    srcp = bass.AP(art, P_XP + kk * 512, [[ARW, 128], [2048, 8], [1, 512]])
                    dbc = bass.AP(v_dsk[:, :].tensor, 0, [[8, 128], [1, 8], [0, 512]])
                    return e.tensor_tensor(srcp, srcp, dbc, ALU.mult)
                dsc_steps.append(sch.add("v", dsc, deps=[t_wb, ld_dsk]))

        for tc in range(16):
            tg = b * 16 + tc
            xbuf = A_XT[tc % 2]
            xt_ap = arf(xbuf, 128, 1024)
            dep = []
            if tc >= 2:
                dep += [u_done[tc - 2], rt_mm[tc - 2], sumx_s[tc - 2],
                        sumq_s[tc - 2], xfm_ev[tc - 2]]
            elif b == 0:
                dep += [apad_done[-1]]
            else:
                dep += prev_b["rank_wbc_done"] + [prev_b["sumx_last"],
                                                  prev_b["sumq_last"],
                                                  prev_b["u_last"],
                                                  prev_b["rt_last"]]
            xt_dma[tc] = sch.add("y", lambda e, xt_ap=xt_ap, tg=tg: e.dma_start(
                xt_ap, bass.AP(x_h, tg * 128 * 1024, [[1024, 128], [1, 1024]])),
                deps=dep, dma=True)
            sumx_s[tc] = sch.add("v", lambda e, xt_ap=xt_ap, tg=tg: e.tensor_reduce(
                st_sumx[:, tg:tg + 1], xt_ap, AX.X, ALU.add), deps=[xt_dma[tc]])
            junk = arb(A_JUNK, 128, 1024)
            dep_q = [xt_dma[tc], z_sg]
            if tc:
                dep_q.append(sumq_s[tc - 1])
            elif b:
                dep_q.append(prev_b["sumq_last"])
            def sq_f(e, xt_ap=xt_ap, tg=tg, junk=junk):
                e.activation(junk, xt_ap, AF.Square, accum_out=st_sumq[:, tg:tg + 1])
                return e.activation(vdum[:, 4:6], vdum[:, 2:4], AF.Identity)
            sumq_s[tc] = sch.add("s", sq_f, deps=dep_q)

            xfm = arf(A_XFM[tc % 2], 128, 1024)
            def xtp(e, xbuf=xbuf):
                last = None
                for c in range(8):
                    srcp = bass.AP(art, xbuf + 2 * 128 * c, [[ARW, 128], [1, 256]]).bitcast(F32)
                    last = e.transpose(PS5[:, c * 128:(c + 1) * 128], srcp, cid_f[:])
                return last
            t_x = sch.add("t", xtp, deps=[xt_dma[tc], ps5_last, lc["id_f"]])
            t_x_s[tc] = t_x
            v_xe = sch.add("v", lambda e, xfm=xfm: e.tensor_copy(xfm, PS5[:, :]),
                           deps=[t_x] + ([rt_mm[tc - 2]] if tc >= 2 else
                                         (prev_b.get("rtmm_tail", []) if b else [])))
            xfm_ev[tc] = v_xe
            ps5_last = v_xe
            slot = tc % 3
            def rmm(e, xfm=xfm, slot=slot):
                last = None
                for c in range(8):
                    rhs = bass.AP(art, xfm.offset * 2 + 2 * 128 * c,
                                  [[ARW, 128], [1, 256]]).bitcast(F32)
                    last = e.matmul(PS2[0:1, 128 + slot * 128:256 + slot * 128],
                                    v_rw[:, c:c + 1], rhs, start=(c == 0), stop=(c == 7))
                return last
            dep_r = [v_xe, ld_rw, v_rbb]
            if b:
                dep_r += prev_b["ps23_free"]
            key = (b, slot)
            if key in rt_slot_ev:
                dep_r.append(rt_slot_ev[key])
            elif b and (1 - b, slot) in rt_slot_ev:
                dep_r.append(rt_slot_ev[(1 - b, slot)])
            rt_mm[tc] = sch.add("t", rmm, deps=dep_r)
            v_re = sch.add("v", lambda e, slot=slot: e.tensor_copy(
                lrow[0:1, 0:128], PS2[0:1, 128 + slot * 128:256 + slot * 128]),
                deps=[rt_mm[tc], logrow_last])
            d_re = sch.add("g", lambda e, tg=tg: e.dma_start(
                bass.AP(rowa_h, tg * 128, [[1, 1], [1, 128]]), lrow[0:1, 0:128]),
                deps=[v_re], dma=True, q="gr")
            rt_ev[tc] = d_re
            rt_slot_ev[(b, slot)] = v_re
            logrow_last = d_re
            if "dbgst" in debug_names and b == 0 and tc == 0:
                dbg_cp1 = sch.add("v", lambda e: e.tensor_copy(
                    dbg_sb[:, 16:144], arf(A_XFM[0], 128, 128)), deps=[v_xe])
                dbg_cp2 = sch.add("v", lambda e: e.tensor_copy(
                    dbg_sb[0:1, 160:288], lrow[0:1, 0:128]), deps=[v_re])
                dbg_probe_sts.extend([dbg_cp1, dbg_cp2])

            if tc % 2 == 1:
                sl = slice(tg - 1, tg + 1)
                def sm1(e, sl=sl):
                    return CH(e,
                        lambda e: e.tensor_scalar(st_mu0[:, sl], st_sumx[:, sl], 1.0 / 1024.0, None, op0=ALU.mult),
                        lambda e: e.tensor_scalar(st_wrk[:, 0:2], st_sumq[:, sl], 1.0 / 1024.0, None, op0=ALU.mult),
                        lambda e: e.tensor_tensor(st_wrk[:, 2:4], st_mu0[:, sl], st_mu0[:, sl], ALU.mult),
                        lambda e: e.tensor_tensor(st_wrk[:, 0:2], st_wrk[:, 0:2], st_wrk[:, 2:4], ALU.subtract),
                        lambda e: e.tensor_scalar(st_wrk[:, 4:6], st_wrk[:, 0:2], 1e-5, None, op0=ALU.add))
                m1 = sch.add("v", sm1, deps=[sumx_s[tc - 1], sumx_s[tc],
                                             sumq_s[tc - 1], sumq_s[tc]])
                if "dbgst" in debug_names and b == 0 and tc == 1:
                    dbg_probe_sts.append(sch.add("v", lambda e: e.tensor_copy(
                        dbg_sb[:, 12:14], st_sumq[:, 0:2]), deps=[sumq_s[1]]))
                    dbg_probe_sts.append(sch.add("s", lambda e: e.activation(
                        dbg_sb[:, 14:16], st_sumq[:, 0:2], AF.Identity),
                        deps=[sumq_s[1]]))
                    dbg_probe_sts.append(sch.add("v", lambda e: e.tensor_copy(
                        dbg_sb[:, 288:290], st_sumx[:, 0:2]), deps=[sumx_s[1]]))
                    dbg_probe_sts.append(sch.add("v", lambda e: e.tensor_copy(
                        dbg_sb[:, 0:8], st_wrk[:, 0:8]), deps=[m1]))
                s1 = sch.add("s", lambda e: e.activation(st_wrk[:, 6:8], st_wrk[:, 4:6], AF.Sqrt),
                             deps=[m1])
                if "dbgst" in debug_names and b == 0 and tc == 1:
                    dbg_probe_sts.append(sch.add("s", lambda e: e.activation(
                        dbg_sb[:, 8:10], st_wrk[:, 6:8], AF.Identity), deps=[s1]))
                def sm2(e, sl=sl):
                    return CH(e,
                        lambda e: e.reciprocal(st_r0[:, sl], st_wrk[:, 6:8]),
                        lambda e: e.tensor_tensor(st_wrk[:, 8:10], st_r0[:, sl], st_r0[:, sl], ALU.mult),
                        lambda e: e.tensor_tensor(st_wrk[:, 8:10], st_wrk[:, 8:10], st_wrk[:, 4:6], ALU.mult),
                        lambda e: e.tensor_scalar(st_wrk[:, 8:10], st_wrk[:, 8:10], -0.5, 1.5, op0=ALU.mult, op1=ALU.add),
                        lambda e: e.tensor_tensor(st_r0[:, sl], st_r0[:, sl], st_wrk[:, 8:10], ALU.mult),
                        lambda e: e.tensor_tensor(st_wrk[:, 10:12], st_r0[:, sl], st_r0[:, sl], ALU.mult),
                        lambda e: e.tensor_tensor(st_wrk[:, 10:12], st_wrk[:, 10:12], st_wrk[:, 0:2], ALU.mult),
                        lambda e: e.tensor_scalar(st_wrk[:, 12:14], st_wrk[:, 10:12], 1e-5, None, op0=ALU.add))
                m2 = sch.add("v", sm2, deps=[s1])
                if "dbgst" in debug_names and b == 0 and tc == 1:
                    dbg_probe_sts.append(sch.add("v", lambda e, sl=sl: e.tensor_copy(
                        dbg_sb[:, 10:12], st_r0[:, sl]), deps=[m2]))
                s2 = sch.add("s", lambda e: e.activation(st_wrk[:, 14:16], st_wrk[:, 12:14], AF.Sqrt),
                             deps=[m2])
                def sm3(e, sl=sl):
                    return CH(e,
                        lambda e: e.reciprocal(st_wrk[:, 16:18], st_wrk[:, 14:16]),
                        lambda e: e.tensor_tensor(st_wrk[:, 18:20], st_wrk[:, 16:18], st_wrk[:, 16:18], ALU.mult),
                        lambda e: e.tensor_tensor(st_wrk[:, 18:20], st_wrk[:, 18:20], st_wrk[:, 12:14], ALU.mult),
                        lambda e: e.tensor_scalar(st_wrk[:, 18:20], st_wrk[:, 18:20], -0.5, 1.5, op0=ALU.mult, op1=ALU.add),
                        lambda e: e.tensor_tensor(st_wrk[:, 16:18], st_wrk[:, 16:18], st_wrk[:, 18:20], ALU.mult),
                        lambda e: e.tensor_tensor(st_alp[:, sl], st_r0[:, sl], st_wrk[:, 16:18], ALU.mult),
                        lambda e: e.tensor_tensor(st_nma[:, sl], st_mu0[:, sl], st_alp[:, sl], ALU.mult),
                        lambda e: e.tensor_scalar(st_nma[:, sl], st_nma[:, sl], -1.0, None, op0=ALU.mult))
                m3 = sch.add("v", sm3, deps=[s2])
                if "dbgst" in debug_names and b == 0 and tc == 1:
                    dbg_probe_sts.append(sch.add("v", lambda e: e.tensor_copy(
                        dbg_sb[:, 290:292], st_sumq[:, 0:2]), deps=[m3]))
                stat_done[tc - 1] = stat_done[tc] = m3
                emit_tail(tc - 1)
                emit_tail(tc)

        passa_tail = [u_done[15], ufm_ev[15], rt_ev[15], rt_ev[14], rt_ev[13],
                      xfm_ev[15], xz_ev[-1], xz_ev[-2], dsc_steps[-1], sumq_s[15]]
        dump(f"mu0_{b}", lambda b=b: st_mu0[:, b * 16:(b + 1) * 16], [128, 16], F32,
             [stat_done[15]])
        dump(f"sumx_{b}", lambda b=b: st_sumx[:, b * 16:(b + 1) * 16], [128, 16], F32,
             [stat_done[15]])
        dump(f"xt15_{b}", lambda: arf(A_XT[1], 128, 1024), [128, 1024], F32,
             [xt_dma[15]])
        dump(f"sumq_{b}", lambda b=b: st_sumq[:, b * 16:(b + 1) * 16], [128, 16], F32,
             [stat_done[15]])
        dump(f"r0_{b}", lambda b=b: st_r0[:, b * 16:(b + 1) * 16], [128, 16], F32,
             [stat_done[15]])
        dump(f"alp_{b}", lambda b=b: st_alp[:, b * 16:(b + 1) * 16], [128, 16], F32,
             [stat_done[15]])
        dump(f"ufm_{b}", lambda: arb(A_UFM, 128, 4096), [128, 4096], F32,
             [ufm_ev[15]])
        dump(f"xproj_{b}", lambda: bass.AP(art, P_XP, [[ARW, 128], [1, 2048]]),
             [128, 2048], F32, [dsc_steps[-1]])
        dump(f"zgc_{b}", lambda: bass.AP(art, P_ZG, [[ARW, 128], [1, 2048]]),
             [128, 2048], F32, [xz_ev[-1], dsc_steps[-1]])
        dump(f"zg_{b}", lambda: bass.AP(art, P_ZG, [[ARW, 128], [1, 2048]]),
             [128, 2048], F32, [xz_ev[-1], dsc_steps[-1]])
        dump(f"brev_{b}", lambda: bass.AP(art, A_BREV, [[ARW, 16], [1, 2048]]),
             [16, 2048], F32, [brev_ev[-1]])

        def bttp(e):
            last = None
            for jp in range(16):
                src = bass.AP(art, A_BREV + jp * 128, [[ARW, 16], [1, 128]])
                j = 15 - jp
                last = e.transpose(
                    bass.AP(PS4[:, :].tensor, j * 16, [[1024, 128], [1, 16]]),
                    src, cid_bf[0:16, 0:16])
            return last
        t_bt = sch.add("t", bttp, deps=brev_ev + [ps4_last])
        v_bt = sch.add("v", lambda e: e.tensor_copy(
            bass.AP(BT[:, :].tensor, 0, [[256, 128], [1, 16], [16, 16]]),
            bass.AP(PS4[:, :].tensor, 0, [[1024, 128], [1, 256]])), deps=[t_bt])
        ps4_last = v_bt
        dump(f"bt_{b}", lambda: BT[:], [128, 256], F32, [v_bt])

        ld_xw = sch.add("g", lambda e, b=b: e.dma_start(
            rk_w[:, b * 16:(b + 1) * 16],
            bass.AP(rowa_h, b * 2048, [[1, 128], [128, 16]])),
            deps=[rt_ev[i] for i in range(16)], dma=True, q="gr")
        dump(f"xwraw{b}", lambda b=b: rk_w[:, b * 16:(b + 1) * 16], [128, 16], F32, [ld_xw])
        sl = slice(b * 16, (b + 1) * 16)
        def wfix(e, sl=sl):
            return CH(e,
                lambda e: e.scalar_tensor_tensor(rk_w[:, sl], st_mu0[:, sl], v_sm[:, 2:3],
                                                 rk_w[:, sl], op0=ALU.mult, op1=ALU.subtract),
                lambda e: e.tensor_tensor(rk_w[:, sl], rk_w[:, sl], st_r0[:, sl], ALU.mult),
                lambda e: e.tensor_scalar(rk_w[:, sl], rk_w[:, sl], -1.0, v_sm[:, 3:4],
                                          op0=ALU.mult, op1=ALU.add))
        v_fx = sch.add("v", wfix, deps=[ld_xw, v_rwsb, v_rbb, stat_done[15]])
        dump(f"logits{b}", lambda b=b: rk_w[:, b * 16:(b + 1) * 16], [128, 16], F32, [v_fx])
        d_wc = sch.add("g", lambda e, b=b: e.dma_start(
            bass.AP(rowa_h, (2 + b) * 4096, [[1, 128], [128, 16]]),
            rk_w[:, b * 16:(b + 1) * 16]), deps=[v_fx], dma=True, q="gr")
        wbc_ev = []
        for piece in range(4):
            ldp = sch.add("g", lambda e, piece=piece, b=b: e.dma_start(
                lrow[:], bass.AP(rowa_h, (2 + b) * 4096 + piece * 512, [[1, 1], [1, 512]])),
                deps=[d_wc, logrow_last], dma=True, q="gr")
            logrow_last = ldp
            bank = [PS0, PS1, PS2, PS3][piece]
            t_b2 = sch.add("t", lambda e, bank=bank: e.matmul(
                bank[:, :], ONES_R(), lrow[:], start=True, stop=True),
                deps=[ldp] + passa_tail + [v_bt] +
                     (prev_b["ps01_free"] if b else []))
            v_e = sch.add("v", lambda e, piece=piece, bank=bank: e.tensor_copy(
                arf(R_WBC + piece * 1024, 128, 512), bank[:, :]), deps=[t_b2] + passa_tail)
            wbc_ev.append(v_e)
            logrow_last = v_e
        g_prev = [None, None]
        rank_last = None
        rank_mms = []
        for jc in range(16):
            gap = arb(R_GB[jc % 2], 128, 2048)
            dep_g = wbc_ev + [v_fx, g_prev[jc % 2]]
            eng = "v"
            v_g = sch.add(eng, lambda e, gap=gap, jc=jc, b=b: e.tensor_scalar(
                gap, arf(R_WBC, 128, 2048), rk_w[:, b * 16 + jc:b * 16 + jc + 1],
                None, op0=ALU.is_gt), deps=dep_g)
            def rmm2(e, gap=gap, jc=jc):
                last = None
                for piece in range(4):
                    bank = [PS0, PS1, PS2, PS3][piece]
                    rhs = bass.AP(gap.tensor, gap.offset + piece * 512, [[ARW, 128], [1, 512]])
                    last = e.matmul(bank[0:1, :], cones_cb[:], rhs,
                                    start=(jc == 0), stop=(jc == 15))
                return last
            dep_r = [v_g, lc["ones_cb"], rank_last]
            if jc == 0:
                dep_r += wbc_ev
            rank_last = sch.add("t", rmm2, deps=dep_r)
            rank_mms.append(rank_last)
            g_prev[jc % 2] = rank_last
        rk_sts = []
        for piece in range(4):
            bank = [PS0, PS1, PS2, PS3][piece]
            v_e = sch.add("v", lambda e, bank=bank: e.tensor_copy(lrow[:], bank[0:1, :]),
                          deps=[rank_last, logrow_last])
            if piece < 2:
                ps01_last[piece] = v_e
            d_s = sch.add("g", lambda e, b=b, piece=piece: e.dma_start(
                bass.AP(rowa_h, (4 + b) * 4096 + piece * 512, [[1, 1], [1, 512]]), lrow[:]),
                deps=[v_e], dma=True, q="gr")
            logrow_last = d_s
            rk_sts.append(d_s)
        ld_rk = sch.add("g", lambda e, b=b: e.dma_start(
            rk_rank[:, b * 16:(b + 1) * 16],
            bass.AP(rowa_h, (4 + b) * 4096, [[1, 128], [128, 16]])), deps=rk_sts,
            dma=True, q="gr")
        v_sel = sch.add("v", lambda e, b=b: e.tensor_scalar(
            rk_sel[:, b * 16:(b + 1) * 16], rk_rank[:, b * 16:(b + 1) * 16], 1792.5, None,
            op0=ALU.is_gt), deps=[ld_rk])
        t_cs = sch.add("t", lambda e, b=b: e.matmul(
            PS3[:, 0:16], ctri_f[:], rk_sel[:, b * 16:(b + 1) * 16], start=True, stop=True),
            deps=[v_sel, lc["tri_f"]])
        t_tot = sch.add("t", lambda e, b=b: e.matmul(
            PS2[0:1, 32:48], ONES_C(), rk_sel[:, b * 16:(b + 1) * 16],
            start=True, stop=True), deps=[t_cs])
        def offs(e):
            return CH(e,
                lambda e: e.tensor_copy(st_wrk[0:1, 32:48], PS2[0:1, 32:48]),
                lambda e: e.tensor_tensor_scan(st_wrk[0:1, 48:64], st_wrk[0:1, 32:48],
                                               ctri_f[0:1, 0:16], 0.0, op0=ALU.add, op1=ALU.mult),
                lambda e: e.tensor_tensor(st_wrk[0:1, 32:48], st_wrk[0:1, 48:64],
                                          st_wrk[0:1, 32:48], ALU.subtract))
        v_of = sch.add("v", offs, deps=[t_cs, t_tot, lc["tri_f"]])
        t_ob = sch.add("t", lambda e: e.matmul(PS2[:, 0:16], ONES_R(), st_wrk[0:1, 32:48],
                                               start=True, stop=True), deps=[v_of])
        def omf(e, b=b):
            sl = slice(b * 16, (b + 1) * 16)
            cjk = cjk0 if b == 0 else cjk1
            return CH(e,
                lambda e: e.tensor_copy(st_wrk[:, 32:48], PS2[:, 0:16]),
                lambda e: e.tensor_tensor(rk_csum[:, sl], PS3[:, 0:16], st_wrk[:, 32:48], ALU.add),
                lambda e: e.tensor_scalar(rk_omf[:, sl], rk_csum[:, sl], -256.0, None, op0=ALU.add),
                lambda e: e.tensor_tensor(rk_omf[:, sl], rk_omf[:, sl], rk_sel[:, sl], ALU.mult),
                lambda e: e.tensor_scalar(rk_omf[:, sl], rk_omf[:, sl], 255.0 + 256.0 * b, None, op0=ALU.add),
                lambda e: e.tensor_scalar(st_wrk[:, 32:48], rk_sel[:, sl], -1.0, 1.0,
                                          op0=ALU.mult, op1=ALU.add),
                lambda e: e.tensor_tensor(st_wrk[:, 32:48], st_wrk[:, 32:48], cjk[:], ALU.mult),
                lambda e: e.tensor_tensor(rk_omf[:, sl], rk_omf[:, sl], st_wrk[:, 32:48], ALU.add),
                lambda e: e.tensor_scalar(rk_omf[:, sl], rk_omf[:, sl], 1023.0, None, op0=ALU.min),
                lambda e: e.tensor_copy(rk_omi[:, sl], rk_omf[:, sl]))
        v_om = sch.add("v", omf, deps=[t_ob, lc["cjk0"], lc["cjk1"]])
        d_z = sch.add("g", lambda e, b=b: e.dma_start(
            bass.AP(possel_h, b * 256 + 255, [[1, 1], [1, 1]]), czero_i[:]),
            deps=[lc["zi"]], dma=True, q="gr")
        sc_sts = []
        for c in range(16):
            def psc(e, b=b, c=c):
                pay = ciota if b == 0 else ciota2
                return e.indirect_dma_start(
                    bass.AP(possel_h, 0, [[1, 1024], [1, 1]]),
                    IndirectOffsetOnAxis(rk_omi[:, b * 16 + c:b * 16 + c + 1], 0),
                    pay[:, c:c + 1], None)
            sc_sts.append(sch.add("g", psc, deps=[v_om, d_z, z_ps, lc["iota"], lc["iota2"]],
                                  dma=True, q=f"gs{c % 4}"))
        def sgw(e, b=b):
            e.memset(bass.AP(rk_sg[:, :].tensor, 3, [[64, 128], [4, 16]]), 0.0)
            e.tensor_copy(bass.AP(rk_sg[:, :].tensor, 0, [[64, 128], [4, 16]]),
                          rk_w[:, b * 16:(b + 1) * 16])
            e.tensor_copy(bass.AP(rk_sg[:, :].tensor, 1, [[64, 128], [4, 16]]),
                          st_mu0[:, b * 16:(b + 1) * 16])
            return e.tensor_copy(bass.AP(rk_sg[:, :].tensor, 2, [[64, 128], [4, 16]]),
                                 st_r0[:, b * 16:(b + 1) * 16])
        v_sg = sch.add("v", sgw, deps=[v_fx, v_om])
        d_sg = sch.add("g", lambda e, b=b: e.dma_start(
            bass.AP(sg_h, b * 2048 * 4, [[4, 128], [512, 16], [1, 4]]), rk_sg[:]),
            deps=[v_sg, z_sg], dma=True, q="gr")
        ld_ps = sch.add("g", lambda e, b=b: e.dma_start(
            sp_idx[:, b * 2:(b + 1) * 2],
            bass.AP(possel_h, b * 256, [[1, 128], [128, 2]])), deps=sc_sts,
            dma=True, q="gr")
        def betaf(e, b=b):
            sl = slice(b * 16, (b + 1) * 16)
            return CH(e,
                lambda e: e.tensor_scalar(st_wrk[:, 48:64], rk_sel[:, sl], -1.0, 2.0, op0=ALU.mult, op1=ALU.add),
                lambda e: e.tensor_tensor(st_wrk[:, 48:64], st_wrk[:, 48:64], st_r0[:, sl], ALU.mult),
                lambda e: e.tensor_scalar(rk_b1[:, sl], st_wrk[:, 48:64], 1.0, None, op0=ALU.add),
                lambda e: e.tensor_tensor(rk_b2[:, sl], st_wrk[:, 48:64], st_mu0[:, sl], ALU.mult),
                lambda e: e.tensor_scalar(rk_b2[:, sl], rk_b2[:, sl], -1.0, None, op0=ALU.mult))
        v_beta = sch.add("v", betaf, deps=[v_sel, v_om])
        dump(f"sel{b}", lambda b=b: rk_sel[:, b * 16:(b + 1) * 16], [128, 16], F32, [v_sel])
        dump(f"om{b}", lambda b=b: rk_omf[:, b * 16:(b + 1) * 16], [128, 16], F32, [v_om])

        conv_last = None
        h_prev = [None, None]
        for s_i in range(S):
            hbuf = C_HREV[s_i % 2]
            dep_h = [apad_done[s_i], h_prev[s_i % 2]]
            if b == 0 and s_i < 2:
                dep_h += [xz_ev[-1]]
            if b == 1 and s_i < 2:
                dep_h += prev_b["c_done"]
            d_h = sch.add("g", lambda e, s_i=s_i, hbuf=hbuf: e.dma_start(
                arb(hbuf, 128, 2048), bass.AP(apad_h, s_i * 2176 + 1, [[1, 128], [1, 2048]])),
                deps=dep_h, dma=True, q="gh")
            def conv_s(e, s_i=s_i, hbuf=hbuf):
                last = None
                for dd in range(16):
                    lhs = arb(hbuf + 128 * dd, 128, 128)
                    n = 16 - dd
                    rhs = bass.AP(BT[:, :].tensor, s_i * 16, [[256, 128], [1, n]])
                    outp = bass.AP(PS0[:, :].tensor, s_i * 16 + dd, [[512, 128], [1, n]])
                    last = e.matmul(outp, lhs, rhs,
                                    start=(s_i == 0 and dd == 0),
                                    stop=(s_i == S - 1 and dd == 15))
                return last
            dep_c = [d_h, conv_last, v_bt]
            if s_i == 0:
                dep_c += [ld_rk]
            conv_last = sch.add("t", conv_s, deps=dep_c)
            h_prev[s_i % 2] = conv_last
        v_xm = sch.add("v", lambda e: e.tensor_copy(
            bass.AP(XMS[:, :].tensor, 0, [[256, 128], [1, 16], [16, 16]]),
            PS0[:, 0:256]), deps=[conv_last])
        dump(f"xm{b}", lambda b=b: XMS[:], [128, 256], F32, [v_xm])
        xmt_evs = []
        t_prevx = None
        for rnd in range(2):
            def xmtp(e, rnd=rnd):
                last = None
                for i in range(8):
                    g = rnd * 8 + i
                    last = e.transpose(PS4[0:16, i * 128:(i + 1) * 128],
                                       XMS[:, g * 16:(g + 1) * 16], cid_bf[:])
                return last
            t_x2 = sch.add("t", xmtp, deps=[v_xm, ps4_last, t_prevx] + xmt_evs[-1:])
            v_x2 = sch.add("v", lambda e, rnd=rnd: e.tensor_copy(
                bass.AP(XMT[:, :].tensor, rnd * 1024, [[2048, 16], [1, 1024]]),
                bass.AP(PS4[:, :].tensor, 0, [[1024, 16], [1, 1024]])), deps=[t_x2])
            xmt_evs.append(v_x2)
            t_prevx = t_x2
        v_xmt = xmt_evs[-1]
        ps4_last = v_xmt

        gdeps = [ld_ps, d_sg, z_misc]
        if b:
            gdeps += prev_b["b_region_free"]
        gsts = []
        for rc in range(2):
            nrow = 128 if rc == 0 else 127
            idx = sp_idx[0:nrow, b * 2 + rc:b * 2 + rc + 1]
            dst = bass.AP(art, B_XSEL + rc * 2048, [[ARW, nrow], [1, 2048]]).bitcast(F32)
            gsts.append(sch.add("g", lambda e, idx=idx, dst=dst: e.indirect_dma_start(
                dst, None, bass.AP(x_h, 0, [[1024, 4096], [1, 1024]]),
                IndirectOffsetOnAxis(idx, 0)), deps=gdeps, dma=True, q="gg"))
            gsts.append(sch.add("g", lambda e, idx=idx, rc=rc, nrow=nrow: e.indirect_dma_start(
                sp_sg[0:nrow, rc * 4:(rc + 1) * 4], None,
                bass.AP(sg_h, 0, [[4, 4096], [1, 4]]),
                IndirectOffsetOnAxis(idx, 0)), deps=gdeps, dma=True, q="gg"))
            gsts.append(sch.add("g", lambda e, idx=idx, rc=rc, b=b, nrow=nrow: e.indirect_dma_start(
                sp_posi[0:nrow, b * 2 + rc:b * 2 + rc + 1], None,
                bass.AP(pos_h, 0, [[1, 4096], [1, 1]]),
                IndirectOffsetOnAxis(idx, 0)), deps=gdeps, dma=True, q="gg"))
        v_pc = sch.add("v", lambda e, b=b: e.tensor_copy(
            sp_posf[:, b * 2:(b + 1) * 2], sp_posi[:, b * 2:(b + 1) * 2]), deps=gsts)
        v_nm = sch.add("v", lambda e: e.tensor_scalar(
            sp_r1[:, 8:10], bass.AP(sp_sg[:, :].tensor, 1, [[16, 128], [4, 2]]),
            -1.0, None, op0=ALU.mult), deps=gsts)
        nsts = []
        for rc in range(2):
            xs = bass.AP(art, B_XSEL + rc * 2048, [[ARW, 128], [1, 2048]]).bitcast(F32)
            nsts.append(sch.add("v", lambda e, xs=xs, rc=rc: e.tensor_scalar(
                xs, xs, sp_r1[:, 8 + rc:9 + rc],
                bass.AP(sp_sg[:, :].tensor, rc * 4 + 2, [[16, 128], [1, 1]]),
                op0=ALU.add, op1=ALU.mult), deps=[v_nm]))
        dump(f"xnsel{b}", lambda b=b: bass.AP(art, B_XSEL, [[ARW, 128], [1, 2048]]).bitcast(F32),
             [128, 1024], F32, nsts)
        junk = arb(A_JUNK, 128, 1024)
        q1 = []
        for rc in range(2):
            xs = bass.AP(art, B_XSEL + rc * 2048, [[ARW, 128], [1, 2048]]).bitcast(F32)
            def q1_f(e, xs=xs, rc=rc, junk=junk):
                e.activation(junk, xs, AF.Square, accum_out=sp_r1[:, rc:rc + 1])
                return e.activation(vdum[:, 4:6], vdum[:, 2:4], AF.Identity)
            q1.append(sch.add("s", q1_f, deps=nsts + (q1[-1:] if q1 else [sumq_s[15]])))
        v_a1 = sch.add("v", lambda e: e.tensor_scalar(
            sp_r1[:, 0:2], sp_r1[:, 0:2], 1.0 / 1024.0, 1e-6, op0=ALU.mult, op1=ALU.add),
            deps=q1)
        s_a1 = sch.add("s", lambda e: e.activation(sp_r1[:, 2:4], sp_r1[:, 0:2], AF.Sqrt),
                       deps=[v_a1])
        def nwt(e):
            return CH(e,
                lambda e: e.reciprocal(sp_r1[:, 4:6], sp_r1[:, 2:4]),
                lambda e: e.tensor_tensor(sp_r1[:, 6:8], sp_r1[:, 4:6], sp_r1[:, 4:6], ALU.mult),
                lambda e: e.tensor_tensor(sp_r1[:, 6:8], sp_r1[:, 6:8], sp_r1[:, 0:2], ALU.mult),
                lambda e: e.tensor_scalar(sp_r1[:, 6:8], sp_r1[:, 6:8], -0.5, 1.5, op0=ALU.mult, op1=ALU.add),
                lambda e: e.tensor_tensor(sp_r1[:, 4:6], sp_r1[:, 4:6], sp_r1[:, 6:8], ALU.mult))
        v_n1 = sch.add("v", nwt, deps=[s_a1])
        hsts = []
        for rc in range(2):
            xs = bass.AP(art, B_XSEL + rc * 2048, [[ARW, 128], [1, 2048]]).bitcast(F32)
            hnap = arb(B_HN + rc * 1024, 128, 1024)
            hsts.append(sch.add("v", lambda e, xs=xs, hnap=hnap, rc=rc: e.tensor_scalar(
                hnap, xs, sp_r1[:, 4 + rc:5 + rc], None, op0=ALU.mult), deps=[v_n1]))
        hnf = []
        for dc in range(8):
            def htp(e, dc=dc):
                last = None
                for rc in range(2):
                    src = arb(B_HN + rc * 1024 + dc * 128, 128, 128)
                    last = e.transpose(PS4[:, rc * 128:(rc + 1) * 128], src, cid_bf[:])
                return last
            t_h = sch.add("t", htp, deps=hsts + [ps4_last])
            v_h = sch.add("s", lambda e, dc=dc: e.activation(
                arb(B_HNF + dc * 256, 128, 256),
                bass.AP(PS4[:, :].tensor, 0, [[1024, 128], [1, 256]]),
                AF.Identity, scale=v_ln1[:, dc:dc + 1]), deps=[t_h, ld_ln1])
            hnf.append(v_h)
            ps4_last = v_h

        def load_w1024(h_t, off, deps):
            return sch.add("g", lambda e: e.dma_start(
                bass.AP(wart, off, [[16384, 128], [1024, 8], [1, 1024]]),
                bass.AP(h_t, 0, [[1024, 128], [128 * 1024, 8], [1, 1024]])),
                deps=deps, dma=True, q="gw")

        war_free = xz_ev[-4:] + dsc_steps[-1:] + [t for t in (rank_mms[-1],)]
        ld_q = load_w1024(wq_h, 0, war_free + (prev_b["war_free"] if b else []))
        ld_k = load_w1024(wk_h, 8192, [ld_q])

        def proj(dst_off, war_off, scale_step, dep_w):
            evs = []
            for ec in range(8):
                bank, bi = (PS0, 0) if ec % 2 == 0 else (PS1, 1)
                def pmm(e, ec=ec, war_off=war_off, bank=bank):
                    last = None
                    for c in range(8):
                        lhs = bass.AP(wart, war_off + c * 1024 + ec * 128,
                                      [[16384, 128], [1, 128]])
                        rhs = arb(B_HNF + c * 256, 128, 255)
                        last = e.matmul(bank[:, 0:255], lhs, rhs, start=(c == 0), stop=(c == 7))
                    return last
                dep_p = dep_w + hnf + [ps01_last[bi]]
                if dst_off == B_QF and ec < 2:
                    dep_p += [v_xm, conv_last]
                t_p = sch.add("t", pmm, deps=dep_p)
                if scale_step:
                    v_p = sch.add("s", lambda e, ec=ec, bank=bank, dst_off=dst_off: e.activation(
                        arb(dst_off + ec * 256, 128, 255), bank[:, 0:255],
                        AF.Identity, scale=cq8[:]), deps=[t_p, lc["q8"]])
                else:
                    v_p = sch.add("v", lambda e, ec=ec, bank=bank, dst_off=dst_off: e.tensor_copy(
                        arb(dst_off + ec * 256, 128, 255), bank[:, 0:255]), deps=[t_p])
                ps01_last[bi] = v_p
                evs.append(v_p)
            return evs
        q_evs = proj(B_QF, 0, True, [ld_q])
        k_evs = proj(B_KF, 8192, False, [ld_k])
        ld_v = load_w1024(wv_h, 0, q_evs)
        v_evs = proj(B_VF, 0, False, [ld_v])

        rope_tab = []
        for rc in range(2):
            ang = arf(B_ANG, 128, 128)
            ang2 = arf(B_ANG + 256, 128, 128)
            ki = arf(B_SCR, 128, 128).bitcast(I32)
            kf = arf(B_SCR + 256, 128, 128)
            def angf(e, rc=rc, b=b, ang=ang, ang2=ang2, ki=ki, kf=kf):
                return CH(e,
                    lambda e: e.tensor_scalar(ang, cinv2[:], sp_posf[:, b * 2 + rc:b * 2 + rc + 1],
                                              None, op0=ALU.mult),
                    lambda e: e.tensor_scalar(ang2, ang, INV2PI, None, op0=ALU.mult),
                    lambda e: e.tensor_copy(ki, ang2),
                    lambda e: e.tensor_copy(kf, ki),
                    lambda e: e.scalar_tensor_tensor(ang, kf, N2PI, ang, op0=ALU.mult, op1=ALU.add),
                    lambda e: e.tensor_scalar(kf, ang, PI, None, op0=ALU.is_gt),
                    lambda e: e.scalar_tensor_tensor(ang, kf, N2PI, ang, op0=ALU.mult, op1=ALU.add),
                    lambda e: e.tensor_scalar(ang2, ang, PI / 2.0, None, op0=ALU.add),
                    lambda e: e.tensor_scalar(kf, ang2, PI, None, op0=ALU.is_gt),
                    lambda e: e.scalar_tensor_tensor(ang2, kf, N2PI, ang2, op0=ALU.mult, op1=ALU.add))
            v_an = sch.add("v", angf, deps=[v_pc, lc["inv2"]] + rope_tab)
            sin_t = arb(B_SCR, 128, 128)
            cos_t = arb(B_SCR + 128, 128, 128)
            s_s = sch.add("s", lambda e, ang=ang, sin_t=sin_t: e.activation(
                sin_t, ang, AF.Sin), deps=[v_an])
            s_c = sch.add("s", lambda e, ang2=ang2, cos_t=cos_t: e.activation(
                cos_t, ang2, AF.Sin), deps=[v_an, s_s])
            def rtp(e, sin_t=sin_t, cos_t=cos_t):
                e.transpose(PS4[:, 0:128], sin_t, cid_bf[:])
                return e.transpose(PS4[:, 128:256], cos_t, cid_bf[:])
            t_r = sch.add("t", rtp, deps=[s_s, s_c, ps4_last])
            def rev_(e, rc=rc):
                e.tensor_copy(arb(B_SIN + rc * 128, 128, 128),
                              bass.AP(PS4[:, :].tensor, 0, [[1024, 128], [1, 128]]))
                return e.tensor_copy(arb(B_COS + rc * 128, 128, 128),
                                     bass.AP(PS4[:, :].tensor, 128, [[1024, 128], [1, 128]]))
            v_r = sch.add("v", rev_, deps=[t_r])
            rope_tab.append(v_r)
            ps4_last = v_r
        rope_done = []
        ps2_free = [v_rbb]
        for (off, evs) in ((B_QF, q_evs), (B_KF, k_evs)):
            for ec in range(8):
                def rotmm(e, off=off, ec=ec):
                    return e.matmul(PS2[:, 0:255], crot[:], arb(off + ec * 256, 128, 255),
                                    start=True, stop=True)
                dep_rm = [evs[ec], lc["rot"]] + rope_tab + rope_done[-1:]
                if not rope_done:
                    dep_rm += [v_om]
                t_rm = sch.add("t", rotmm, deps=dep_rm)
                def rap(e, off=off, ec=ec):
                    tcos = arb(B_SCR, 128, 255)
                    tsin = arb(B_SCR + 256, 128, 255)
                    return CH(e,
                        lambda e: e.tensor_tensor(tcos, arb(off + ec * 256, 128, 255),
                                                  arb(B_COS, 128, 255), ALU.mult),
                        lambda e: e.tensor_tensor(tsin, PS2[:, 0:255], arb(B_SIN, 128, 255), ALU.mult),
                        lambda e: e.tensor_tensor(arb(off + ec * 256, 128, 255), tcos, tsin, ALU.add))
                v_ra = sch.add("v", rap, deps=[t_rm])
                rope_done.append(v_ra)
        o_done = []
        sc_evs = []
        for hh in range(16):
            ec = hh // 2
            po = (hh % 2) * 64
            sc_evs = []
            for qc in range(2):
                nq = 128 if qc == 0 else 127
                def smm2(e, ec=ec, po=po, qc=qc, nq=nq):
                    lhs = bass.AP(art, po * ARW + B_QF + ec * 256 + qc * 128,
                                  [[ARW, 64], [1, nq]])
                    rhs = bass.AP(art, po * ARW + B_KF + ec * 256, [[ARW, 64], [1, 255]])
                    return e.matmul(PS3[0:nq, 0:255], lhs, rhs, start=True, stop=True)
                dep_s = rope_done + (o_done[-1:] if o_done else []) + sc_evs[-1:]
                if hh == 0 and qc == 0:
                    dep_s += [v_om, v_of]
                t_sc = sch.add("t", smm2, deps=dep_s)
                nmx = sp_r1[:, 10:11]
                v_mx = sch.add("v", lambda e, nq=nq, nmx=nmx: e.reduce_max(
                    nmx[0:nq, :], PS3[0:nq, 0:255], axis=AX.X, negate=True), deps=[t_sc])
                eb = bass.AP(art, B_EB + qc * 256, [[ARW, nq], [1, 255]])
                s_ex = sch.add("s", lambda e, eb=eb, nq=nq, nmx=nmx: e.activation(
                    eb, PS3[0:nq, 0:255], AF.Exp, bias=nmx[0:nq, :]), deps=[v_mx])
                trit = ctri0 if qc == 0 else ctri1
                g_tz = sch.add("g", lambda e, eb=eb, nq=nq, trit=trit: e.tensor_tensor(
                    eb, eb, trit[0:nq, 0:255], ALU.mult),
                    deps=[s_ex, lc["tri0"], lc["tri1"]])
                smv = sp_r1[:, 11:12]
                v_s2 = sch.add("v", lambda e, eb=eb, nq=nq, smv=smv: e.reduce_sum(
                    smv[0:nq, :], eb, axis=AX.X), deps=[g_tz])
                def nrm(e, eb=eb, nq=nq, smv=smv):
                    return CH(e,
                        lambda e: e.reciprocal(sp_r1[0:nq, 12:13], smv[0:nq, :]),
                        lambda e: e.tensor_scalar(eb, eb, sp_r1[0:nq, 12:13], None, op0=ALU.mult))
                v_nr = sch.add("v", nrm, deps=[v_s2])
                sc_evs.append(v_nr)
            def atp(e):
                last = None
                for qc in range(2):
                    nq = 128 if qc == 0 else 127
                    src0 = bass.AP(art, B_EB + qc * 256, [[ARW, nq], [1, 128]])
                    last = e.transpose(PS4[0:128, qc * 128:qc * 128 + nq], src0,
                                       cid_bf[0:nq, 0:nq])
                    src1 = bass.AP(art, B_EB + qc * 256 + 128, [[ARW, nq], [1, 127]])
                    last = e.transpose(PS4[0:127, 512 + qc * 128:512 + qc * 128 + nq], src1,
                                       cid_bf[0:nq, 0:nq])
                return last
            t_at = sch.add("t", atp, deps=sc_evs + [ps4_last])
            def atev(e):
                e.tensor_copy(arb(B_AT, 128, 255),
                              bass.AP(PS4[:, :].tensor, 0, [[1024, 128], [1, 255]]))
                e.tensor_copy(arb(B_AT + 256, 127, 255),
                              bass.AP(PS4[:, :].tensor, 512, [[1024, 127], [1, 255]]))
                return e.tensor_scalar(vdum[:, 0:2], vdum[:, 2:4], 1.0, None, op0=ALU.mult)
            v_at = sch.add("v", atev, deps=[t_at])
            ps4_last = v_at
            def vtp(e, ec=ec, po=po):
                src0 = bass.AP(art, po * ARW + B_VF + ec * 256, [[ARW, 64], [1, 128]])
                e.transpose(PS4[0:128, 768:832], src0, cid_bf[po:po + 64, po:po + 64])
                src1 = bass.AP(art, po * ARW + B_VF + ec * 256 + 128, [[ARW, 64], [1, 127]])
                return e.transpose(PS4[0:127, 832:896], src1, cid_bf[po:po + 64, po:po + 64])
            t_vt = sch.add("t", vtp, deps=v_evs + [v_at])
            def vtev(e):
                e.tensor_copy(arb(B_VT, 128, 64),
                              bass.AP(PS4[:, :].tensor, 768, [[1024, 128], [1, 64]]))
                e.tensor_copy(arb(B_VT + 64, 127, 64),
                              bass.AP(PS4[:, :].tensor, 832, [[1024, 127], [1, 64]]))
                return e.tensor_scalar(vdum[:, 0:2], vdum[:, 2:4], 1.0, None, op0=ALU.mult)
            v_vt = sch.add("v", vtev, deps=[t_vt])
            ps4_last = v_vt
            def omm(e, po=po):
                e.matmul(PS2[po:po + 64, 0:255], arb(B_VT, 128, 64), arb(B_AT, 128, 255),
                         start=True, stop=False)
                return e.matmul(PS2[po:po + 64, 0:255], arb(B_VT + 64, 127, 64),
                                arb(B_AT + 256, 127, 255), start=False, stop=True)
            dep_o = [v_at, v_vt] + rope_done[-1:]
            t_o = sch.add("t", omm, deps=dep_o)
            def oev(e, ec=ec, po=po):
                e.tensor_copy(
                    bass.AP(art, po * ARW + B_KF + ec * 256, [[ARW, 64], [1, 255]]),
                    PS2[po:po + 64, 0:255])
                return e.tensor_scalar(vdum[:, 0:2], vdum[:, 2:4], 1.0, None, op0=ALU.mult)
            v_o = sch.add("v", oev, deps=[t_o] + k_evs + rope_done)
            o_done.append(v_o)
        dump(f"hnf_{b}", lambda: arb(B_HNF, 128, 2048), [128, 2048], F32, hnf)
        dump(f"qf_{b}", lambda: arb2(B_QF, 128, [[256, 8], [1, 255]]), [128, 2040], F32, rope_done)
        dump(f"kf_{b}", lambda: arb2(B_KF, 128, [[256, 8], [1, 255]]), [128, 2040], F32, rope_done)
        dump(f"eb_{b}", lambda: arb(B_EB, 128, 255), [128, 255], F32, sc_evs)
        dump(f"ofm_{b}", lambda: arb2(B_KF, 128, [[256, 8], [1, 255]]), [128, 2040], F32, o_done)
        ld_o = load_w1024(wo_h, 8192, o_done[:4])
        xo_evs = []
        for ec in range(8):
            bank, bi = (PS0, 0) if ec % 2 == 0 else (PS1, 1)
            def wmm2(e, ec=ec, bank=bank):
                last = None
                for c in range(8):
                    lhs = bass.AP(wart, 8192 + c * 1024 + ec * 128, [[16384, 128], [1, 128]])
                    rhs = arb(B_KF + c * 256, 128, 255)
                    last = e.matmul(bank[:, 0:255], lhs, rhs, start=(c == 0), stop=(c == 7))
                return last
            t_w = sch.add("t", wmm2, deps=[ld_o] + o_done + [ps01_last[bi]])
            v_w = sch.add("v", lambda e, ec=ec, bank=bank: e.tensor_copy(
                arb(B_QF + ec * 256, 128, 255), bank[:, 0:255]),
                deps=[t_w] + rope_done + q_evs)
            ps01_last[bi] = v_w
            xo_evs.append(v_w)
        dump(f"xofm_{b}", lambda: arb2(B_QF, 128, [[256, 8], [1, 255]]), [128, 2040], F32, xo_evs)
        x1_done = []
        for rc in range(2):
            nq = 128 if rc == 0 else 127
            def xotp(e, rc=rc, nq=nq):
                last = None
                for dc in range(8):
                    src = bass.AP(art, B_QF + dc * 256 + rc * 128, [[ARW, 128], [1, nq]])
                    last = e.transpose(PS4[0:nq, dc * 128:dc * 128 + 128], src, cid_bf[:])
                return last
            t_xo = sch.add("t", xotp, deps=xo_evs + [ps4_last])
            xs = bass.AP(art, B_XSEL + rc * 2048, [[ARW, nq], [1, 2048]]).bitcast(F32)
            x1ap = bass.AP(art, B_PROC + rc * 2048, [[ARW, nq], [1, 2048]]).bitcast(F32)
            v_x1 = sch.add("v", lambda e, xs=xs, x1ap=x1ap, nq=nq: e.tensor_add(
                x1ap, xs, bass.AP(PS4[:, :].tensor, 0, [[1024, nq], [1, 1024]])),
                deps=[t_xo] + hsts + (prev_b["proc_free"] if b and rc == 0 else []))
            x1_done.append(v_x1)
            ps4_last = v_x1
        dump(f"x1_{b}", lambda: bass.AP(art, B_PROC, [[ARW, 128], [1, 2048]]).bitcast(F32),
             [128, 1024], F32, x1_done)
        q2 = []
        for rc in range(2):
            xs = bass.AP(art, B_PROC + rc * 2048, [[ARW, 128], [1, 2048]]).bitcast(F32)
            def q2_f(e, xs=xs, junk=junk, rc=rc):
                e.activation(junk, xs, AF.Square, accum_out=sp_r1[:, rc:rc + 1])
                return e.activation(vdum[:, 4:6], vdum[:, 2:4], AF.Identity)
            q2.append(sch.add("s", q2_f, deps=x1_done + (q2[-1:] if q2 else [])))
        v_a2 = sch.add("v", lambda e: e.tensor_scalar(
            sp_r1[:, 0:2], sp_r1[:, 0:2], 1.0 / 1024.0, 1e-6, op0=ALU.mult, op1=ALU.add),
            deps=q2)
        s_a2 = sch.add("s", lambda e: e.activation(sp_r1[:, 2:4], sp_r1[:, 0:2], AF.Sqrt),
                       deps=[v_a2])
        v_n2 = sch.add("v", nwt, deps=[s_a2])
        x1n = []
        for rc in range(2):
            xs = bass.AP(art, B_PROC + rc * 2048, [[ARW, 128], [1, 2048]]).bitcast(F32)
            hnap = arb(B_HN + rc * 1024, 128, 1024)
            x1n.append(sch.add("v", lambda e, xs=xs, hnap=hnap, rc=rc: e.tensor_scalar(
                hnap, xs, sp_r1[:, 4 + rc:5 + rc], None, op0=ALU.mult),
                deps=[v_n2] + hnf))
        hnf2 = []
        for dc in range(8):
            def htp2(e, dc=dc):
                last = None
                for rc in range(2):
                    src = arb(B_HN + rc * 1024 + dc * 128, 128, 128)
                    last = e.transpose(PS4[:, rc * 128:(rc + 1) * 128], src, cid_bf[:])
                return last
            t_h = sch.add("t", htp2, deps=x1n + [ps4_last])
            def hnf2_f(e, dc=dc):
                dst = arb(B_HNF + dc * 256, 128, 256)
                e.activation(dst, bass.AP(PS4[:, :].tensor, 0, [[1024, 128], [1, 256]]),
                             AF.Identity, scale=v_ln2[:, dc:dc + 1])
                return e.activation(vdum[:, 4:6], vdum[:, 2:4], AF.Identity)
            v_h = sch.add("s", hnf2_f, deps=[t_h, ld_ln2] + hnf)
            hnf2.append(v_h)
            ps4_last = v_h
        dump(f"hnf2_{b}", lambda: arb(B_HNF, 128, 2048), [128, 2048], F32, hnf2)
        w1_slots = [0, 1024, 2048, 3072]
        w2_slots = [4096, 5120]
        gl_done = []
        x2_last = None
        ld_w1p = {}
        w2_mms = []
        for fc in range(32):
            sl1 = w1_slots[fc % 4]
            dep1 = ([ld_o] + xo_evs) if fc < 4 else [gl_done[fc - 4]]
            ld_w1p[fc] = sch.add("g", lambda e, fc=fc, sl1=sl1: e.dma_start(
                bass.AP(wart, sl1, [[16384, 128], [128, 8], [1, 128]]),
                bass.AP(w1_h, fc * 128, [[4096, 128], [128 * 4096, 8], [1, 128]])),
                deps=dep1, dma=True, q="gw")
            sl2 = w2_slots[fc % 2]
            dep2 = [ld_w1p[fc]] if fc < 2 else [ld_w1p[fc], gl_done[fc - 2],
                                               w2_mms[fc - 2]]
            ld_w2p = sch.add("g", lambda e, fc=fc, sl2=sl2: e.dma_start(
                bass.AP(wart, sl2, [[16384, 128], [1, 1024]]),
                bass.AP(w2_h, fc * 128 * 1024, [[1024, 128], [1, 1024]])),
                deps=dep2, dma=True, q="gw")
            bank, bi = (PS0, 0) if fc % 2 == 0 else (PS1, 1)
            def w1mm(e, sl1=sl1, bank=bank):
                last = None
                for c in range(8):
                    lhs = bass.AP(wart, sl1 + c * 128, [[16384, 128], [1, 128]])
                    rhs = arb(B_HNF + c * 256, 128, 255)
                    last = e.matmul(bank[:, 0:255], lhs, rhs, start=(c == 0), stop=(c == 7))
                return last
            t_1 = sch.add("t", w1mm, deps=[ld_w1p[fc]] + hnf2 + [ps01_last[bi]])
            h1ap = arb(B_H1 + (fc % 2) * 256, 128, 255)
            def gel_f(e, h1ap=h1ap, bank=bank, fc=fc):
                e.activation(h1ap, bank[:, 0:255], AF.Gelu_apprx_tanh)
                return e.activation(vdum[:, 4:6], vdum[:, 2:4], AF.Identity)
            s_g = sch.add("s", gel_f, deps=[t_1])
            gl_done.append(s_g)
            ps01_last[bi] = s_g
            def w2mm(e, fc=fc, sl2=sl2, h1ap=h1ap):
                last = None
                for rc in range(2):
                    nq = 128 if rc == 0 else 127
                    lhs = bass.AP(h1ap.tensor, h1ap.offset + rc * 128, [[ARW, 128], [1, nq]])
                    for half in range(2):
                        rhs = bass.AP(wart, sl2 + half * 512, [[16384, 128], [1, 512]])
                        if rc == 0:
                            outp = PS5[0:nq, half * 512:(half + 1) * 512]
                        else:
                            outp = (PS2 if half == 0 else PS3)[0:nq, :]
                        last = e.matmul(outp, lhs, rhs, start=(fc == 0), stop=(fc == 31))
                return last
            dep_w2 = [ld_w2p, s_g, x2_last]
            if fc == 0:
                dep_w2 += [ps5_last, o_done[-1], sc_evs[-1], v_om, xfm_ev[15]]
            x2_last = sch.add("t", w2mm, deps=dep_w2)
            w2_mms.append(x2_last)
        dump(f"h1_{b}", lambda: arb2(B_H1, 128, [[256, 2], [1, 255]]),
             [128, 510], F32, [gl_done[-1]])
        proc_done = []
        for rc in range(2):
            nq = 128 if rc == 0 else 127
            x1ap = bass.AP(art, B_PROC + rc * 2048, [[ARW, nq], [1, 2048]]).bitcast(F32)
            wcol = bass.AP(sp_sg[:, :].tensor, rc * 4, [[16, nq], [1, 1]])
            v_p1 = sch.add("v", lambda e, x1ap=x1ap, wcol=wcol: e.tensor_scalar(
                x1ap, x1ap, wcol, None, op0=ALU.mult), deps=[x2_last] + x1n)
            def pr2(e, rc=rc, nq=nq, wcol=wcol):
                for half in range(2):
                    dstf = bass.AP(art, B_PROC + rc * 2048 + half * 1024,
                                   [[ARW, nq], [1, 1024]]).bitcast(F32)
                    if rc == 0:
                        src = PS5[0:nq, half * 512:(half + 1) * 512]
                    else:
                        src = (PS2 if half == 0 else PS3)[0:nq, :]
                    ins = e.scalar_tensor_tensor(dstf, src, wcol, dstf,
                                                 op0=ALU.mult, op1=ALU.add)
                return ins
            v_p2 = sch.add("v", pr2, deps=[v_p1])
            proc_done.append(v_p2)
        dump(f"procB{b}",
             lambda b=b: bass.AP(art, B_PROC + 2048, [[ARW, 127], [1, 2048]]).bitcast(F32),
             [127, 1024], F32, proc_done)
        dmp_proc = dump(f"proc{b}",
                        lambda b=b: bass.AP(art, B_PROC, [[ARW, 128], [1, 2048]]).bitcast(F32),
                        [128, 1024], F32, proc_done)
        ps5_last = proc_done[-1]

        ld_wout = load_w1024(woutt_h, 8192, gl_done[-2:] + xo_evs)
        out_writes = []
        gated_last = None
        wc_last = proc_done[-1]
        v_f = None
        gated_all = []
        for cc in range(16):
            tglob = b * 16 + cc
            def wcmm(e, cc=cc):
                last = None
                for ec in range(8):
                    lhs = wct_sb[0:16, ec * 128:(ec + 1) * 128]
                    rhs = bass.AP(XMT[:, :].tensor, cc * 128, [[2048, 16], [1, 128]])
                    last = e.matmul(PS5[:, ec * 128:(ec + 1) * 128], lhs, rhs,
                                    start=(ec in (0, 4)), stop=(ec in (3, 7)))
                return last
            t_wc = sch.add("t", wcmm, deps=[v_xmt, ld_wct, wc_last])
            def gat(e, cc=cc):
                off = cc * 128
                xv = bass.AP(art, P_XP + off, [[ARW, 128], [2048, 8], [1, 128]])
                zv = bass.AP(art, P_ZG + off, [[ARW, 128], [2048, 8], [1, 128]])
                g8 = bass.AP(art, C_GATED, [[ARW, 128], [128, 8], [1, 128]])
                ps = bass.AP(PS5[:, :].tensor, 0, [[1024, 128], [128, 8], [1, 128]])
                return CH(e,
                          lambda e: e.tensor_tensor(g8, ps, xv, ALU.add),
                          lambda e: e.tensor_tensor(g8, g8, zv, ALU.mult))
            v_g = sch.add("v", gat, deps=[t_wc, gated_last])
            gated_last = v_g
            gated_all.append(v_g)
            wc_last = v_g
            def wotmm(e):
                last = None
                for half in range(2):
                    bank = PS0 if half == 0 else PS1
                    for fcx in range(8):
                        lhs = bass.AP(art, C_GATED + fcx * 128, [[ARW, 128], [1, 128]])
                        rhs = bass.AP(wart, 8192 + fcx * 1024 + half * 512,
                                      [[16384, 128], [1, 512]])
                        last = e.matmul(bank[:, :], lhs, rhs, start=(fcx == 0), stop=False)
                    last = e.matmul(bank[:, :], cones_rb[:],
                                    bout_rb[0:1, half * 512:(half + 1) * 512],
                                    start=False, stop=True)
                return last
            dep_wo = [v_g, ld_wout, ps01_last[0], ps01_last[1], ld_bout, lc["ones_rb"]]
            t_wo = sch.add("t", wotmm, deps=dep_wo)
            xre = arf(C_XRE[cc % 2], 128, 1024)
            dep_x = [out_writes[-2]] if len(out_writes) >= 2 else \
                (prev_b["c_done"] if b else [])
            d_xre = sch.add("s", lambda e, xre=xre, tglob=tglob: e.dma_start(
                xre, bass.AP(x_h, tglob * 128 * 1024, [[1024, 128], [1, 1024]])),
                deps=dep_x, dma=True)
            outt = arf(C_OUTT[cc % 2], 128, 1024)
            dep_as = [d_xre, v_beta]
            if len(out_writes) >= 2:
                dep_as.append(out_writes[-2])
            elif b:
                dep_as += prev_b["c_done"]
            v_as = sch.add("v", lambda e, outt=outt, xre=xre, tglob=tglob: e.tensor_scalar(
                outt, xre, rk_b1[:, tglob:tglob + 1], rk_b2[:, tglob:tglob + 1],
                op0=ALU.mult, op1=ALU.add), deps=dep_as)
            def fin(e, outt=outt):
                e.tensor_add(bass.AP(art, outt.offset * 2, [[ARW, 128], [1, 1024]]).bitcast(F32),
                             bass.AP(art, outt.offset * 2, [[ARW, 128], [1, 1024]]).bitcast(F32),
                             PS0[:, :])
                e.tensor_add(
                    bass.AP(art, outt.offset * 2 + 1024, [[ARW, 128], [1, 1024]]).bitcast(F32),
                    bass.AP(art, outt.offset * 2 + 1024, [[ARW, 128], [1, 1024]]).bitcast(F32),
                    PS1[:, :])
                return e.tensor_scalar(vdum[:, 0:2], vdum[:, 2:4], 1.0, None, op0=ALU.mult)
            v_f = sch.add("v", fin, deps=[v_as, t_wo])
            ps01_last[0] = v_f
            ps01_last[1] = v_f
            d_out = sch.add("y", lambda e, outt=outt, tglob=tglob: e.dma_start(
                bass.AP(out_h, tglob * 128 * 1024, [[1024, 128], [1, 1024]]), outt),
                deps=[v_f], dma=True, q="yo")
            out_writes.append(d_out)
        dump(f"gated{b}", lambda: arb2(C_GATED, 128, [[128, 8], [1, 128]]),
             [128, 1024], F32, gated_all[-1:])
        dump(f"outt0_{b}", lambda: arf(C_OUTT[0], 128, 1024), [128, 1024], F32,
             [v_f] + out_writes[-2:])
        dump(f"outt1_{b}", lambda: arf(C_OUTT[1], 128, 1024), [128, 1024], F32,
             [v_f] + out_writes[-1:])
        sc2 = []
        for rc in range(2):
            nrow = 128 if rc == 0 else 127
            idx = sp_idx[0:nrow, b * 2 + rc:b * 2 + rc + 1]
            src = bass.AP(art, B_PROC + rc * 2048, [[ARW, nrow], [1, 2048]]).bitcast(F32)
            sc2.append(sch.add("g", lambda e, idx=idx, src=src: e.indirect_dma_start(
                bass.AP(out_h, 0, [[1024, 4096], [1, 1024]]),
                IndirectOffsetOnAxis(idx, 0), src, None, compute_op=ALU.add),
                deps=out_writes + proc_done, dma=True, q="gp"))
        scatter_all += sc2

        ps5_last = gated_all[-1]
        prev_b = dict(
            ps01_free=[v_f],
            ps23_free=[proc_done[-1]],
            rank_wbc_done=[rank_mms[-1], wbc_ev[-1]],
            sumq_last=sumq_s[15],
            sumx_last=sumx_s[15],
            u_last=u_done[15],
            rt_last=rt_mm[15],
            rtmm_tail=[rt_mm[14], rt_mm[15]],
            gated_all=gated_all,
            brev_last=brev_ev[-1],
            bt_done=v_bt,
            c_done=sc2 + [out_writes[-1], v_f],
            b_region_free=[proc_done[-1], v_g],
            war_free=[t_wo],
            proc_free=sc2 + ([dmp_proc] if dmp_proc else []),
        )

    if "dbgst" in debug_names:
        dump("dbgst", lambda: dbg_sb[:, :], [128, 320], F32, dbg_probe_sts)

    with nc.allow_non_contiguous_dma(reason="small strided vector/row loads"):
        with nc.Block() as block:
            sch.emit(block, sems)
    ctx.pop_all()
    return nc, dump_handles


_CACHE = {}


def kernel(**inputs):
    if "nc" not in _CACHE:
        _CACHE["nc"], _CACHE["dumps"] = build_program(debug_names=_DEBUG_NAMES)
    nc = _CACHE["nc"]
    x = np.ascontiguousarray(np.asarray(inputs["L0_attn_x"], np.float32))
    f32 = lambda a: np.ascontiguousarray(np.asarray(a, np.float32))
    i32 = lambda a: np.ascontiguousarray(np.asarray(a, np.int32))
    W_in = f32(inputs["L0_attn_W_in"])
    W_out = f32(inputs["L0_attn_W_out"])
    W_B = f32(inputs["L0_attn_W_B"])
    W_C = f32(inputs["L0_attn_W_C"])
    base = {
        "wint": np.ascontiguousarray(W_in.T),
        "woutt": np.ascontiguousarray(W_out.T),
        "wbt": np.ascontiguousarray(W_B.T),
        "wct": np.ascontiguousarray(W_C.T),
        "wq": f32(inputs["L0_ffn_wq"]),
        "wk": f32(inputs["L0_ffn_wk"]),
        "wv": f32(inputs["L0_ffn_wv"]),
        "wo": f32(inputs["L0_ffn_wo"]),
        "w1": f32(inputs["L0_ffn_w1"]),
        "w2": f32(inputs["L0_ffn_w2"]),
        "rw": f32(inputs["L0_ffn_router_w"]).reshape(D),
        "rb": f32(inputs["L0_ffn_router_b"]).reshape(1),
        "lng": f32(inputs["L0_attn_ln_g"]),
        "lnb": f32(inputs["L0_attn_ln_b"]),
        "binv": f32(inputs["L0_attn_b_in"]),
        "boutv": f32(inputs["L0_attn_b_out"]),
        "dskip": f32(inputs["L0_attn_D_skip"]),
        "logdt": f32(inputs["L0_attn_log_dt"]),
        "alog": f32(inputs["L0_attn_A_log"]),
        "ln1": f32(inputs["L0_ffn_ln1"]),
        "ln2": f32(inputs["L0_ffn_ln2"]),
    }
    pos = i32(inputs["L0_ffn_position_ids"])
    in_maps = []
    for core in range(8):
        m = dict(base)
        m["x"] = np.ascontiguousarray(x[2 * core:2 * core + 2].reshape(NT, D))
        pr = np.stack([pos[min(2 * core + j, 3), :L] for j in range(2)])
        m["posr"] = np.ascontiguousarray(pr)
        in_maps.append(m)
    res = run_bass_kernel_spmd(nc, in_maps, core_ids=list(range(8)))
    _CACHE["last_results"] = res.results
    out = np.concatenate([r["out"].reshape(2, L, D) for r in res.results], axis=0)
    return out


_DEBUG_NAMES = ()


# version 04 (final)
# speedup vs baseline: 1.0351x; 1.0351x over previous
import numpy as np
import ml_dtypes
import concourse.bass as bass
import concourse.mybir as mybir
from concourse.bass_utils import run_bass_kernel_spmd
from concourse.bass import IndirectOffsetOnAxis

F32 = mybir.dt.float32
BF16 = mybir.dt.bfloat16
I32 = mybir.dt.int32
AF = mybir.ActivationFunctionType
ALU = mybir.AluOpType
AX = mybir.AxisListType
bfd = ml_dtypes.bfloat16

D = 1024
L = 2048
NB = 2
NT = 4096
NTILE = 32
S = 16
PI = float(np.pi)
LOG1E8 = float(np.log(1e-8))
LOG1M = float(np.log1p(-1e-8))
INV2PI = float(1.0 / (2.0 * np.pi))
N2PI = float(-2.0 * np.pi)
ARW = 71808


class Sched:
    def __init__(self, race_mode=False):
        self.steps = []
        self.last_dma = {}
        self.race_mode = race_mode
        self.last_nondma = {}

    def add(self, eng, fn, deps=(), dma=False, q=None):
        deps = [d for d in deps if d is not None]
        sid = len(self.steps)
        for d in deps:
            assert 0 <= d < sid, (d, sid)
        key = (q or (eng + "d")) if dma else eng
        if dma:
            prev = self.last_dma.get(key)
            if prev is not None and prev not in deps:
                deps = deps + [prev]
            self.last_dma[key] = sid
        p = self.last_nondma.get(eng)
        if p is not None and p not in deps:
            deps = deps + [p]
        if not dma:
            self.last_nondma[eng] = sid
        self.steps.append(dict(eng=eng, fn=fn, deps=deps, dma=dma, key=key))
        return sid

    def _key(self, st):
        return st["key"]

    def sem_keys(self):
        return sorted({self._key(st) for st in self.steps})

    def emit(self, block, sems):
        needed = set()
        for st in self.steps:
            needed.update(st["deps"])
        cum = {}
        for i, st in enumerate(self.steps):
            k = self._key(st)
            if st["dma"] or i in needed:
                inc = 16 if st["dma"] else 1
                cum[k] = cum.get(k, 0) + inc
                st["inc"] = inc
                st["val"] = cum[k]
            else:
                st["inc"] = 0
                st["val"] = None
        final_vals = dict(cum)

        def run_engine(eng_obj, eng_name):
            if eng_name in ("s", "g"):
                eng_obj.wait_ge(sems["v"], 1)
            waited = {}
            for i, st in enumerate(self.steps):
                if st["eng"] != eng_name:
                    continue
                k_self = self._key(st)
                for d in st["deps"]:
                    dst = self.steps[d]
                    dk = self._key(dst)
                    v = dst["val"]
                    if waited.get(dk, -1) < v:
                        eng_obj.wait_ge(sems[dk], v)
                        waited[dk] = v
                ins = st["fn"](eng_obj)
                if st["inc"]:
                    assert ins is not None, f"step {i} ({eng_name}) returned None"
                    ins.then_inc(sems[k_self], st["inc"])

        block.tensor(lambda e: run_engine(e, "t"))
        block.vector(lambda e: run_engine(e, "v"))
        block.scalar(lambda e: run_engine(e, "s"))
        block.gpsimd(lambda e: run_engine(e, "g"))

        def _sync(y):
            run_engine(y, "y")
            for k, v in final_vals.items():
                y.wait_ge(sems[k], v)
        block.sync(_sync)


def _consts():
    c = {}
    c["id_bf"] = np.eye(128, dtype=bfd)
    c["id_f32"] = np.eye(128, dtype=np.float32)
    c["tri_f32"] = (np.arange(128)[:, None] <= np.arange(128)[None, :]).astype(np.float32)
    c["ones_128x1_bf"] = np.ones((128, 1), bfd)
    c["ones_1x128_bf"] = np.ones((1, 128), bfd)
    e0 = np.zeros((128, 145), np.float32)
    e0[:, 0:128] = np.arange(128, dtype=np.float32)[None, :]
    for j in range(129, 145):
        e0[:, j] = 128.0 * (j - 129)
    c["e0"] = e0
    io = (np.arange(16)[None, :] * 128 + np.arange(128)[:, None]).astype(np.int32)
    c["iota_col"] = io
    c["iota_col2"] = io + 2048
    inv = (10000.0 ** (-(np.arange(128) % 32) * 2.0 / 64.0)).astype(np.float32)
    inv = np.where((np.arange(128) % 64) < 32,
                   (10000.0 ** (-(np.arange(128) % 64) * 2.0 / 64.0)).astype(np.float32),
                   (10000.0 ** (-((np.arange(128) % 64) - 32) * 2.0 / 64.0)).astype(np.float32))
    c["inv2"] = np.broadcast_to(inv[None, :], (128, 128)).copy().astype(np.float32)
    R = np.zeros((128, 128), bfd)
    for B in (0, 64):
        for j in range(32):
            R[B + 32 + j, B + j] = -1.0
            R[B + j, B + 32 + j] = 1.0
    c["rot"] = R
    for qc in (0, 1):
        r = np.arange(128)[:, None] + 128 * qc
        cc = np.arange(256)[None, :]
        t = (cc <= r).astype(bfd)
        t[:, 255] = 0
        c[f"tri{qc}"] = t
    c["q8"] = np.full((128, 1), 0.125, np.float32)
    c["cjnk0"] = np.broadcast_to((257.0 + np.arange(16, dtype=np.float32))[None, :],
                                 (128, 16)).copy()
    c["cjnk1"] = np.broadcast_to((17.0 + np.arange(16, dtype=np.float32))[None, :],
                                 (128, 16)).copy()
    c["zero_i32"] = np.zeros((1, 1), np.int32)
    return c


DEBUG_DUMPS = []


def build_program(debug_names=(), detect_races=False):
    nc = bass.Bass("TRN2", target_bir_lowering=False,
                   detect_race_conditions=detect_races)

    x_h = nc.dram_tensor("x", [NT, D], F32, kind="ExternalInput")
    wint_h = nc.dram_tensor("wint", [D, 2 * D], F32, kind="ExternalInput")
    woutt_h = nc.dram_tensor("woutt", [D, D], F32, kind="ExternalInput")
    wbt_h = nc.dram_tensor("wbt", [D, S], F32, kind="ExternalInput")
    wct_h = nc.dram_tensor("wct", [S, D], F32, kind="ExternalInput")
    wq_h = nc.dram_tensor("wq", [D, D], F32, kind="ExternalInput")
    wk_h = nc.dram_tensor("wk", [D, D], F32, kind="ExternalInput")
    wv_h = nc.dram_tensor("wv", [D, D], F32, kind="ExternalInput")
    wo_h = nc.dram_tensor("wo", [D, D], F32, kind="ExternalInput")
    w1_h = nc.dram_tensor("w1", [D, 4 * D], F32, kind="ExternalInput")
    w2_h = nc.dram_tensor("w2", [4 * D, D], F32, kind="ExternalInput")
    rw_h = nc.dram_tensor("rw", [D], F32, kind="ExternalInput")
    rb_h = nc.dram_tensor("rb", [1], F32, kind="ExternalInput")
    lng_h = nc.dram_tensor("lng", [D], F32, kind="ExternalInput")
    lnb_h = nc.dram_tensor("lnb", [D], F32, kind="ExternalInput")
    bin_h = nc.dram_tensor("binv", [2 * D], F32, kind="ExternalInput")
    bout_h = nc.dram_tensor("boutv", [D], F32, kind="ExternalInput")
    dskip_h = nc.dram_tensor("dskip", [D], F32, kind="ExternalInput")
    logdt_h = nc.dram_tensor("logdt", [D], F32, kind="ExternalInput")
    alog_h = nc.dram_tensor("alog", [D, S], F32, kind="ExternalInput")
    ln1_h = nc.dram_tensor("ln1", [D], F32, kind="ExternalInput")
    ln2_h = nc.dram_tensor("ln2", [D], F32, kind="ExternalInput")
    pos_h = nc.dram_tensor("posr", [NB, L], I32, kind="ExternalInput")
    out_h = nc.dram_tensor("out", [NT, D], F32, kind="ExternalOutput")

    CN = _consts()
    c_h = {k: nc.inline_tensor(v, name="c_" + k) for k, v in CN.items()}

    apad_h = nc.dram_tensor("d_apad", [S, 2176], BF16, kind="Internal")
    sg_h = nc.dram_tensor("d_sg", [NB * L, 4], F32, kind="Internal")
    possel_h = nc.dram_tensor("d_possel", [1024], I32, kind="Internal")
    rowa_h = nc.dram_tensor("d_rowa", [8, 4096], F32, kind="Internal")

    from contextlib import ExitStack
    ctx = ExitStack()

    def sbt(name, shape, dtype):
        return ctx.enter_context(nc.sbuf_tensor(name, shape, dtype))

    def pst(name, shape, dtype=F32):
        return ctx.enter_context(nc.psum_tensor(name, shape, dtype))

    cid_bf = sbt("cid_bf", [128, 128], BF16)
    cid_f = sbt("cid_f", [128, 128], F32)
    ctri_f = sbt("ctri_f", [128, 128], F32)
    cones_cb = sbt("cones_cb", [128, 1], BF16)
    cones_rb = sbt("cones_rb", [1, 128], BF16)
    ce0 = sbt("ce0", [128, 145], F32)
    ciota = sbt("ciota", [128, 16], I32)
    ciota2 = sbt("ciota2", [128, 16], I32)
    cinv2 = sbt("cinv2", [128, 128], F32)
    crot = sbt("crot", [128, 128], BF16)
    ctri0 = sbt("ctri0", [128, 256], BF16)
    ctri1 = sbt("ctri1", [128, 256], BF16)
    cq8 = sbt("cq8", [128, 1], F32)
    cjk0 = sbt("cjk0", [128, 16], F32)
    cjk1 = sbt("cjk1", [128, 16], F32)
    czero_i = sbt("czero_i", [1, 1], I32)

    ONES_R = lambda n=128: ctri_f[0:1, 0:n]
    ONES_C = lambda: ctri_f[:, 127:128]

    v_lng = sbt("v_lng", [128, 8], F32)
    v_lnbb = sbt("v_lnbb", [128, 8], BF16)
    v_bin = sbt("v_bin", [128, 16], F32)
    v_dsk = sbt("v_dsk", [128, 8], F32)
    v_logdt = sbt("v_logdt", [128, 8], F32)
    v_ln1 = sbt("v_ln1", [128, 8], F32)
    v_ln2 = sbt("v_ln2", [128, 8], F32)
    v_rw = sbt("v_rw", [128, 8], F32)
    v_rb = sbt("v_rb", [1, 1], F32)
    v_la = sbt("v_la", [128, 128], F32)
    v_dt = sbt("v_dt", [128, 8], F32)
    v_ndt = sbt("v_ndt", [128, 8], F32)
    v_dtm16 = sbt("v_dtm16", [16, 1], F32)
    v_sm = sbt("v_sm", [128, 8], F32)
    v_bb = sbt("v_bb", [128, 16], F32)
    wbt_sb = sbt("wbt_sb", [128, 128], BF16)
    wct_sb = sbt("wct_sb", [16, 1024], BF16)
    bout_rb = sbt("bout_rb", [1, 1024], BF16)
    lrow = sbt("lrow", [1, 512], F32)

    st_sumx = sbt("st_sumx", [128, 32], F32)
    st_sumq = sbt("st_sumq", [128, 32], F32)
    st_mu0 = sbt("st_mu0", [128, 32], F32)
    st_r0 = sbt("st_r0", [128, 32], F32)
    st_alp = sbt("st_alp", [128, 32], F32)
    st_nma = sbt("st_nma", [128, 32], F32)
    st_wrk = sbt("st_wrk", [128, 64], F32)
    dbg_sb = sbt("dbg_sb", [128, 320], F32)
    dbg_probe_sts = []
    vdum = sbt("vdum", [128, 8], F32)

    _spc = [0]

    def SPB(e, ins):
        _spc[0] += 1
        ins.then_inc(sems["sp"], 1)
        e.wait_ge(sems["sp"], _spc[0])

    def CH(e, *fns):
        ins = None
        for f in fns:
            if ins is not None:
                SPB(e, ins)
            ins = f(e)
        return ins

    rk_w = sbt("rk_w", [128, 32], F32)
    rk_rank = sbt("rk_rank", [128, 32], F32)
    rk_sel = sbt("rk_sel", [128, 32], F32)
    rk_csum = sbt("rk_csum", [128, 32], F32)
    rk_omf = sbt("rk_omf", [128, 32], F32)
    rk_omi = sbt("rk_omi", [128, 32], I32)
    rk_sg = sbt("rk_sg", [128, 64], F32)
    rk_b1 = sbt("rk_b1", [128, 32], F32)
    rk_b2 = sbt("rk_b2", [128, 32], F32)

    sp_idx = sbt("sp_idx", [128, 4], I32)
    sp_posi = sbt("sp_posi", [128, 4], I32)
    sp_posf = sbt("sp_posf", [128, 4], F32)
    sp_sg = sbt("sp_sg", [128, 16], F32)
    sp_r1 = sbt("sp_r1", [128, 16], F32)

    WAR = sbt("WAR", [128, 16384], BF16)
    BT = sbt("BT", [128, 256], BF16)
    XMS = sbt("XMS", [128, 256], BF16)
    XMT = sbt("XMT", [16, 2048], BF16)
    AR = sbt("AR", [128, ARW], BF16)

    PS0 = pst("PS0", [128, 512], F32)
    PS1 = pst("PS1", [128, 512], F32)
    PS2 = pst("PS2", [128, 512], F32)
    PS3 = pst("PS3", [128, 512], F32)
    PS4 = pst("PS4", [128, 1024], BF16)
    PS5 = pst("PS5", [128, 1024], F32)
    PS6 = pst("PS6", [17, 512], F32)
    PS7 = PS6

    SEM_KEYS = ["t", "v", "s", "g", "gd", "yd", "sd", "gw", "gc", "ga", "gh",
                "gr", "gg", "gp", "yo", "gs0", "gs1", "gs2", "gs3", "sp"]
    sems = {k: ctx.enter_context(nc.semaphore("sem_" + k)) for k in SEM_KEYS}
    sch = Sched(race_mode=detect_races)
    art = AR[:, :].tensor
    wart = WAR[:, :].tensor

    def arb(col0, parts, cols):
        return bass.AP(art, col0, [[ARW, parts], [1, cols]])

    def arb2(col0, parts, dims):
        return bass.AP(art, col0, [[ARW, parts]] + dims)

    def arf(col0, parts, elems):
        return bass.AP(art, col0, [[ARW, parts], [1, 2 * elems]]).bitcast(F32)

    def arf2(col0, parts, dims):
        return bass.AP(art, col0, [[ARW, parts]] + dims).bitcast(F32)

    dump_handles = {}

    def dump(name, ap_fn, shape, dtype, deps):
        if name not in debug_names:
            return None
        h = nc.dram_tensor("dbg_" + name, shape, dtype, kind="ExternalOutput")
        dump_handles[name] = h
        return sch.add("g", lambda e: e.dma_start(h.ap(), ap_fn()), deps=deps,
                       dma=True, q="gc")

    P_XP = 0
    P_ZG = 16384
    A_XT = [32768, 34816]
    A_XFM = [36864, 38912]
    A_UT = [40960, 41984]
    A_UFM = 43008
    A_JUNK = 48128
    A_BREV = 49152
    R_WBC = 32768
    R_GB = [36864, 38912]
    C_HREV = [58496, 60544]
    B_XSEL = 36864
    B_PROC = 40960
    B_HN = 45056
    B_HNF = 47104
    B_QF = 49152
    B_KF = 51200
    B_VF = 53248
    B_COS = 55296
    B_SIN = 55552
    B_EB = 55808
    B_AT = 56320
    B_VT = 56832
    B_H1 = 56960
    B_SCR = 57472
    B_ANG = 57984
    C_OUTT = [62592, 64640]
    C_GATED = 66688
    C_XRE = [67712, 69760]
    TBL_X = 32768
    TBL_ES = 35104

    def gld(dst, src, deps=(), q="gc"):
        return sch.add("g", lambda e, d=dst, s=src: e.dma_start(d, s), deps=deps,
                       dma=True, q=q)

    lc = {}
    lc["id_bf"] = gld(cid_bf[:], c_h["id_bf"].ap())
    lc["id_f"] = gld(cid_f[:], c_h["id_f32"].ap())
    lc["tri_f"] = gld(ctri_f[:], c_h["tri_f32"].ap())
    lc["ones_cb"] = gld(cones_cb[:], c_h["ones_128x1_bf"].ap())
    lc["ones_rb"] = gld(cones_rb[:], c_h["ones_1x128_bf"].ap())
    lc["e0"] = gld(ce0[:], c_h["e0"].ap())
    lc["iota"] = gld(ciota[:], c_h["iota_col"].ap())
    lc["iota2"] = gld(ciota2[:], c_h["iota_col2"].ap())
    lc["inv2"] = gld(cinv2[:], c_h["inv2"].ap())
    lc["rot"] = gld(crot[:], c_h["rot"].ap())
    lc["tri0"] = gld(ctri0[:], c_h["tri0"].ap())
    lc["tri1"] = gld(ctri1[:], c_h["tri1"].ap())
    lc["q8"] = gld(cq8[:], c_h["q8"].ap())
    lc["cjk0"] = gld(cjk0[:], c_h["cjnk0"].ap())
    lc["cjk1"] = gld(cjk1[:], c_h["cjnk1"].ap())
    lc["zi"] = gld(czero_i[:], c_h["zero_i32"].ap())

    def fmcols(h, nch):
        return bass.AP(h, 0, [[1, 128], [128, nch]])

    ld_lng = gld(v_lng[:], fmcols(lng_h, 8))
    ld_lnbb = gld(v_lnbb[:], fmcols(lnb_h, 8))
    ld_bin = gld(v_bin[:], fmcols(bin_h, 16))
    ld_dsk = gld(v_dsk[:], fmcols(dskip_h, 8))
    ld_logdt = gld(v_logdt[:], fmcols(logdt_h, 8))
    ld_ln1 = gld(v_ln1[:], fmcols(ln1_h, 8))
    ld_ln2 = gld(v_ln2[:], fmcols(ln2_h, 8))
    ld_rw = gld(v_rw[:], fmcols(rw_h, 8))
    ld_rb = gld(v_rb[:], bass.AP(rb_h, 0, [[1, 1], [1, 1]]))
    ld_la = gld(v_la[:], bass.AP(alog_h, 0, [[16, 128], [2048, 8], [1, 16]]))
    ld_wbt = gld(wbt_sb[:], bass.AP(wbt_h, 0, [[16, 128], [2048, 8], [1, 16]]))
    ld_wct = gld(wct_sb[:], wct_h.ap())
    ld_bout = gld(bout_rb[:], bass.AP(bout_h, 0, [[1024, 1], [1, 1024]]))
    ld_wint = gld(bass.AP(wart, 0, [[16384, 128], [2048, 8], [1, 2048]]),
                  bass.AP(wint_h, 0, [[2048, 128], [128 * 2048, 8], [1, 2048]]), q="gw")

    def z0f(e):
        e.memset(vdum[:, :], 0.0)
        return e.memset(arf(A_JUNK, 128, 128), 0.0)
    v_z0 = sch.add("v", z0f)
    z_sg = sch.add("g", lambda e: e.dma_start(
        bass.AP(sg_h, 0, [[128, 128], [1, 128]]), arf(A_JUNK, 128, 128)),
        deps=[v_z0], dma=True, q="gr")
    v_z1 = sch.add("v", lambda e: e.memset(lrow[:], 0.0), deps=[v_z0])
    def zmisc(e):
        e.memset(sp_posi[:], 0)
        e.memset(sp_posf[:], 0.0)
        e.memset(sp_sg[:], 0.0)
        e.memset(sp_r1[:], 0.0)
        return e.memset(sp_idx[:], 0)
    z_misc = sch.add("v", zmisc, deps=[v_z0])
    z_ps = sch.add("g", lambda e: e.dma_start(
        bass.AP(possel_h, 0, [[1, 512], [1, 1]]), lrow[:]),
        deps=[v_z1], dma=True, q="gr")

    s_exdt = sch.add("s", lambda e: e.activation(v_dt[:], v_logdt[:], AF.Exp), deps=[ld_logdt])
    v_cldt = sch.add("v", lambda e: e.tensor_scalar(v_dt[:], v_dt[:], 1e-4, 1.0,
                                                    op0=ALU.max, op1=ALU.min), deps=[s_exdt])
    v_negdt = sch.add("v", lambda e: e.tensor_scalar(v_ndt[:], v_dt[:], -1.0, None,
                                                     op0=ALU.mult), deps=[v_cldt])

    def mm_dtm(e):
        last = None
        for c in range(8):
            last = e.matmul(PS7[0:1, 0:1], v_dt[:, c:c + 1], ONES_C(),
                            start=(c == 0), stop=(c == 7))
        return last
    t_dtm = sch.add("t", mm_dtm, deps=[v_cldt, lc["tri_f"]])
    v_dtmv = sch.add("v", lambda e: e.tensor_scalar(v_sm[0:1, 0:1], PS7[0:1, 0:1],
                                                    1.0 / 1024.0, None, op0=ALU.mult),
                     deps=[t_dtm])
    t_dtmb = sch.add("t", lambda e: e.matmul(PS7[0:16, 8:9], ONES_R(16), v_sm[0:1, 0:1],
                                             start=True, stop=True), deps=[v_dtmv])
    v_dtmb = sch.add("v", lambda e: e.tensor_copy(v_dtm16[:], PS7[0:16, 8:9]), deps=[t_dtmb])

    def mm_rws(e):
        last = None
        for c in range(8):
            last = e.matmul(PS7[0:1, 16:17], v_rw[:, c:c + 1], ONES_C(),
                            start=(c == 0), stop=(c == 7))
        return last
    t_rws = sch.add("t", mm_rws, deps=[ld_rw, lc["tri_f"], v_dtmb])
    v_rws = sch.add("v", lambda e: e.tensor_copy(v_sm[0:1, 1:2], PS7[0:1, 16:17]), deps=[t_rws])
    t_rwsb = sch.add("t", lambda e: e.matmul(PS2[:, 0:1], ONES_R(), v_sm[0:1, 1:2],
                                             start=True, stop=True), deps=[v_rws])
    v_rwsb = sch.add("v", lambda e: e.tensor_copy(v_sm[:, 2:3], PS2[:, 0:1]), deps=[t_rwsb])
    t_rbb = sch.add("t", lambda e: e.matmul(PS2[:, 1:2], ONES_R(), v_rb[:],
                                            start=True, stop=True), deps=[ld_rb, v_rwsb])
    v_rbb = sch.add("v", lambda e: e.tensor_copy(v_sm[:, 3:4], PS2[:, 1:2]), deps=[t_rbb])

    s_exal = sch.add("s", lambda e: e.activation(v_la[:], v_la[:], AF.Exp), deps=[ld_la])
    la_steps = []
    for c in range(8):
        la_steps.append(sch.add("v", lambda e, c=c: e.tensor_scalar(
            v_la[:, c * 16:(c + 1) * 16], v_la[:, c * 16:(c + 1) * 16],
            v_ndt[:, c:c + 1], None, op0=ALU.mult), deps=[s_exal, v_negdt]))
    v_clla = sch.add("v", lambda e: e.tensor_scalar(v_la[:], v_la[:], LOG1E8, LOG1M,
                                                    op0=ALU.max, op1=ALU.min), deps=la_steps)

    apad_done = []
    prev_tbl = None
    for s_i in range(S):
        xs = arf2(TBL_X, 128, [[292, 8], [1, 290]])
        in0 = bass.AP(ce0[:, :].tensor, 0, [[145, 128], [0, 8], [1, 145]])
        in1 = bass.AP(v_la[:, :].tensor, s_i, [[128, 128], [16, 8], [0, 145]])
        v_x = sch.add("v", lambda e, xs=xs, in0=in0, in1=in1:
                      e.tensor_tensor(xs, in0, in1, ALU.mult),
                      deps=[v_clla] + ([prev_tbl] if prev_tbl else []))
        esb = arb2(TBL_ES, 128, [[146, 8], [1, 145]])
        s_e = sch.add("s", lambda e, esb=esb, xs=xs: e.activation(esb, xs, AF.Exp), deps=[v_x])
        zc = arb2(TBL_ES + 128, 128, [[146, 8], [1, 1]])
        v_z = sch.add("v", lambda e, zc=zc: e.memset(zc, 0.0), deps=[s_e])

        def es_mm(e):
            last = None
            for c in range(8):
                lhs = arb(TBL_ES + c * 146 + 128, 128, 17)
                rhs = arb(TBL_ES + c * 146, 128, 128)
                last = e.matmul(PS6[0:17, 0:128], lhs, rhs, start=(c == 0), stop=(c == 7))
            return last
        t_mm = sch.add("t", es_mm, deps=[v_z, v_rws, v_dtmb])
        v_ev = sch.add("v", lambda e: e.tensor_scalar(
            arb(TBL_X, 17, 128), PS6[0:17, 0:128], 1.0 / 1024.0, None, op0=ALU.mult),
            deps=[t_mm])
        d_ap = sch.add("g", lambda e, s_i=s_i: e.dma_start(
            bass.AP(apad_h, s_i * 2176, [[128, 17], [1, 128]]), arb(TBL_X, 17, 128)),
            deps=[v_ev], dma=True, q="ga")
        apad_done.append(d_ap)
        prev_tbl = d_ap

    dump("apad", lambda: bass.AP(apad_h, 0, [[2176, 16], [1, 2176]]),
         [16, 2176], F32, [apad_done[-1]])

    logrow_last = None
    bb_sts = []
    for h in range(2):
        def mm_bw(e, h=h):
            last = None
            for c in range(8):
                for h2 in range(2):
                    rhs = bass.AP(wart, c * 2048 + h * 1024 + h2 * 512,
                                  [[16384, 128], [1, 512]])
                    last = e.matmul(PS5[0:1, h2 * 512:(h2 + 1) * 512], v_lnbb[:, c:c + 1],
                                    rhs, start=(c == 0), stop=(c == 7))
            return last
        t_bw = sch.add("t", mm_bw, deps=[ld_wint, ld_lnbb] + bb_sts)
        for piece in range(2):
            v_e = sch.add("v", lambda e, piece=piece: e.tensor_copy(
                lrow[:], PS5[0:1, piece * 512:(piece + 1) * 512]),
                deps=[t_bw, logrow_last, z_ps])
            d_s = sch.add("g", lambda e, h=h, piece=piece: e.dma_start(
                bass.AP(rowa_h, 6 * 4096 + h * 1024 + piece * 512, [[1, 1], [1, 512]]),
                lrow[:]), deps=[v_e], dma=True, q="gr")
            logrow_last = d_s
            bb_sts.append(d_s)
    ld_bwin = sch.add("g", lambda e: e.dma_start(
        v_bb[:], bass.AP(rowa_h, 6 * 4096, [[1, 128], [128, 16]])), deps=bb_sts,
        dma=True, q="gr")
    v_bbadd = sch.add("v", lambda e: e.tensor_add(v_bb[:], v_bb[:], v_bin[:]),
                      deps=[ld_bwin, ld_bin])
    wg_steps = []
    for c in range(8):
        def f(e, c=c):
            w = bass.AP(wart, c * 2048, [[16384, 128], [1, 2048]])
            e.tensor_scalar(w, w, v_lng[:, c:c + 1], None, op0=ALU.mult)
            return e.tensor_scalar(vdum[:, 0:2], vdum[:, 2:4], 1.0, None, op0=ALU.mult)
        wg_steps.append(sch.add("v", f, deps=[ld_wint, ld_lng] + bb_sts))

    ps01_last = {0: None, 1: None}
    ps5_last = bb_sts[-1]
    ps4_last = None
    rt_slot_ev = {}
    prev_b = {}
    scatter_all = []
    for b in range(NB):
        xt_dma = [None] * 16
        u_done = [None] * 16
        ufm_ev = [None] * 16
        rt_mm = [None] * 16
        rt_ev = [None] * 16
        stat_done = [None] * 16
        sumq_s = [None] * 16
        xfm_ev = [None] * 16
        xz_ev = []
        brev_ev = []
        dsc_steps = []
        sumx_s = [None] * 16
        t_x_s = [None] * 16
        last_winmm = None
        if b:
            ld_wint_b = gld(bass.AP(wart, 0, [[16384, 128], [2048, 8], [1, 2048]]),
                            bass.AP(wint_h, 0, [[2048, 128], [128 * 2048, 8], [1, 2048]]),
                            deps=prev_b["war_free"] + prev_b["gated_all"][-1:], q="gw")
            wg_b = []
            for c in range(8):
                def fgs(e, c=c):
                    w = bass.AP(wart, c * 2048, [[16384, 128], [1, 2048]])
                    e.tensor_scalar(w, w, v_lng[:, c:c + 1], None, op0=ALU.mult)
                    return e.tensor_scalar(vdum[:, 0:2], vdum[:, 2:4], 1.0, None, op0=ALU.mult)
                wg_b.append(sch.add("v", fgs, deps=[ld_wint_b, ld_lng]))
            wg_cur = wg_b
        else:
            wg_cur = wg_steps

        def emit_tail(tc):
            nonlocal ps4_last, ps01_last, last_winmm
            tg = b * 16 + tc
            xbuf = A_XT[tc % 2]
            xt_ap = arf(xbuf, 128, 1024)
            ubuf = A_UT[tc % 2]
            u_ap = arb(ubuf, 128, 1024)
            dep_u = [stat_done[tc], xt_dma[tc]]
            if tc >= 2:
                dep_u.append(ufm_ev[tc - 2])
            if b and tc < 2:
                dep_u += prev_b["proc_free"]
            u_done[tc] = sch.add("s", lambda e, u_ap=u_ap, tg=tg, xt_ap=xt_ap: e.activation(
                u_ap, xt_ap, AF.Identity, scale=st_alp[:, tg:tg + 1],
                bias=st_nma[:, tg:tg + 1]), deps=dep_u)
            j = tc % 4
            def utp(e, ubuf=ubuf):
                last = None
                for c in range(8):
                    srcp = bass.AP(art, ubuf + 128 * c, [[ARW, 128], [1, 128]])
                    last = e.transpose(PS4[:, c * 128:(c + 1) * 128], srcp, cid_bf[:])
                return last
            t_u = sch.add("t", utp, deps=[u_done[tc], lc["id_bf"], ps4_last])
            def uev(e, j=j):
                dst = arb2(A_UFM + j * 128, 128, [[512, 8], [1, 128]])
                return e.tensor_copy(dst, bass.AP(PS4[:, :].tensor, 0, [[1024, 128], [1, 1024]]))
            dep_ue = [t_u]
            if last_winmm is not None and j == 0:
                dep_ue.append(last_winmm)
            ufm_ev[tc] = sch.add("v", uev, deps=dep_ue)
            ps4_last = ufm_ev[tc]

            if j == 3:
                kk = tc // 4
                blk = [ufm_ev[tc - i] for i in range(4)] + wg_cur
                if b:
                    blk += prev_b["gated_all"]
                for m in range(16):
                    bank, bi = (PS0, 0) if m % 2 == 0 else (PS1, 1)
                    def wmm(e, m=m, bank=bank):
                        last = None
                        for c in range(8):
                            lhs = bass.AP(wart, c * 2048 + m * 128, [[16384, 128], [1, 128]])
                            rhs = arb(A_UFM + c * 512, 128, 512)
                            last = e.matmul(bank[:, :], lhs, rhs, start=(c == 0), stop=(c == 7))
                        return last
                    t_mm = sch.add("t", wmm, deps=blk + [ps01_last[bi]])
                    last_winmm = t_mm
                    if m < 8:
                        def ev(e, m=m, kk=kk, bank=bank):
                            dst = bass.AP(art, P_XP + m * 2048 + kk * 512,
                                          [[ARW, 128], [1, 512]])
                            return e.tensor_scalar(dst, bank[:, :], v_bb[:, m:m + 1], None,
                                                   op0=ALU.add)
                        e_mm = sch.add("v", ev, deps=[t_mm, v_bbadd])
                    else:
                        def ev(e, m=m, kk=kk, bank=bank):
                            dst = bass.AP(art, P_ZG + (m - 8) * 2048 + kk * 512,
                                          [[ARW, 128], [1, 512]])
                            return e.activation(dst, bank[:, :], AF.Silu, bias=v_bb[:, m:m + 1])
                        e_mm = sch.add("s", ev, deps=[t_mm, v_bbadd])
                    xz_ev.append(e_mm)
                    ps01_last[bi] = e_mm
                def wbmm(e, kk=kk):
                    last = None
                    for c in range(8):
                        rhs = bass.AP(art, P_XP + c * 2048 + kk * 512, [[ARW, 128], [1, 512]])
                        last = e.matmul(PS7[0:16, 0:512], wbt_sb[:, c * 16:(c + 1) * 16], rhs,
                                        start=(c == 0), stop=(c == 7))
                    return last
                dep_wb = [xz_ev[-16 + i] for i in range(8)] + [ld_wbt, v_dtmb,
                                                              apad_done[-1]]
                if brev_ev:
                    dep_wb.append(brev_ev[-1])
                elif b:
                    dep_wb.append(prev_b["brev_last"])
                t_wb = sch.add("t", wbmm, deps=dep_wb)
                tk0 = kk * 512
                def bev(e, tk0=tk0):
                    dst = bass.AP(art, A_BREV + (2047 - tk0), [[ARW, 16], [-1, 512]])
                    e.tensor_scalar(dst, PS7[0:16, 0:512], v_dtm16[:], None, op0=ALU.mult)
                    return e.tensor_scalar(vdum[:, 0:2], vdum[:, 2:4], 1.0, None, op0=ALU.mult)
                brev_ev.append(sch.add("v", bev, deps=[t_wb] +
                               ([prev_b["bt_done"]] if (b and kk == 0) else [])))
                def dsc(e, kk=kk):
                    srcp = bass.AP(art, P_XP + kk * 512, [[ARW, 128], [2048, 8], [1, 512]])
                    dbc = bass.AP(v_dsk[:, :].tensor, 0, [[8, 128], [1, 8], [0, 512]])
                    return e.tensor_tensor(srcp, srcp, dbc, ALU.mult)
                dsc_steps.append(sch.add("v", dsc, deps=[t_wb, ld_dsk]))

        for tc in range(16):
            tg = b * 16 + tc
            xbuf = A_XT[tc % 2]
            xt_ap = arf(xbuf, 128, 1024)
            dep = []
            if tc >= 2:
                dep += [u_done[tc - 2], rt_mm[tc - 2], sumx_s[tc - 2],
                        sumq_s[tc - 2], xfm_ev[tc - 2]]
            elif b == 0:
                dep += [apad_done[-1]]
            else:
                dep += prev_b["rank_wbc_done"] + [prev_b["sumx_last"],
                                                  prev_b["sumq_last"],
                                                  prev_b["u_last"],
                                                  prev_b["rt_last"]]
            xt_dma[tc] = sch.add("y", lambda e, xt_ap=xt_ap, tg=tg: e.dma_start(
                xt_ap, bass.AP(x_h, tg * 128 * 1024, [[1024, 128], [1, 1024]])),
                deps=dep, dma=True)
            sumx_s[tc] = sch.add("v", lambda e, xt_ap=xt_ap, tg=tg: e.tensor_reduce(
                st_sumx[:, tg:tg + 1], xt_ap, AX.X, ALU.add), deps=[xt_dma[tc]])
            junk = arb(A_JUNK, 128, 1024)
            dep_q = [xt_dma[tc], z_sg]
            if tc:
                dep_q.append(sumq_s[tc - 1])
            elif b:
                dep_q.append(prev_b["sumq_last"])
            def sq_f(e, xt_ap=xt_ap, tg=tg, junk=junk):
                e.activation(junk, xt_ap, AF.Square, accum_out=st_sumq[:, tg:tg + 1])
                return e.activation(vdum[:, 4:6], vdum[:, 2:4], AF.Identity)
            sumq_s[tc] = sch.add("s", sq_f, deps=dep_q)

            xfm = arf(A_XFM[tc % 2], 128, 1024)
            def xtp(e, xbuf=xbuf):
                last = None
                for c in range(8):
                    srcp = bass.AP(art, xbuf + 2 * 128 * c, [[ARW, 128], [1, 256]]).bitcast(F32)
                    last = e.transpose(PS5[:, c * 128:(c + 1) * 128], srcp, cid_f[:])
                return last
            t_x = sch.add("t", xtp, deps=[xt_dma[tc], ps5_last, lc["id_f"]])
            t_x_s[tc] = t_x
            v_xe = sch.add("v", lambda e, xfm=xfm: e.tensor_copy(xfm, PS5[:, :]),
                           deps=[t_x] + ([rt_mm[tc - 2]] if tc >= 2 else
                                         (prev_b.get("rtmm_tail", []) if b else [])))
            xfm_ev[tc] = v_xe
            ps5_last = v_xe
            slot = tc % 3
            def rmm(e, xfm=xfm, slot=slot):
                last = None
                for c in range(8):
                    rhs = bass.AP(art, xfm.offset * 2 + 2 * 128 * c,
                                  [[ARW, 128], [1, 256]]).bitcast(F32)
                    last = e.matmul(PS2[0:1, 128 + slot * 128:256 + slot * 128],
                                    v_rw[:, c:c + 1], rhs, start=(c == 0), stop=(c == 7))
                return last
            dep_r = [v_xe, ld_rw, v_rbb]
            if b:
                dep_r += prev_b["ps23_free"]
            key = (b, slot)
            if key in rt_slot_ev:
                dep_r.append(rt_slot_ev[key])
            elif b and (1 - b, slot) in rt_slot_ev:
                dep_r.append(rt_slot_ev[(1 - b, slot)])
            rt_mm[tc] = sch.add("t", rmm, deps=dep_r)
            v_re = sch.add("v", lambda e, slot=slot: e.tensor_copy(
                lrow[0:1, 0:128], PS2[0:1, 128 + slot * 128:256 + slot * 128]),
                deps=[rt_mm[tc], logrow_last])
            d_re = sch.add("g", lambda e, tg=tg: e.dma_start(
                bass.AP(rowa_h, tg * 128, [[1, 1], [1, 128]]), lrow[0:1, 0:128]),
                deps=[v_re], dma=True, q="gr")
            rt_ev[tc] = d_re
            rt_slot_ev[(b, slot)] = v_re
            logrow_last = d_re
            if "dbgst" in debug_names and b == 0 and tc == 0:
                dbg_cp1 = sch.add("v", lambda e: e.tensor_copy(
                    dbg_sb[:, 16:144], arf(A_XFM[0], 128, 128)), deps=[v_xe])
                dbg_cp2 = sch.add("v", lambda e: e.tensor_copy(
                    dbg_sb[0:1, 160:288], lrow[0:1, 0:128]), deps=[v_re])
                dbg_probe_sts.extend([dbg_cp1, dbg_cp2])

            if tc % 2 == 1:
                sl = slice(tg - 1, tg + 1)
                def sm1(e, sl=sl):
                    return CH(e,
                        lambda e: e.tensor_scalar(st_mu0[:, sl], st_sumx[:, sl], 1.0 / 1024.0, None, op0=ALU.mult),
                        lambda e: e.tensor_scalar(st_wrk[:, 0:2], st_sumq[:, sl], 1.0 / 1024.0, None, op0=ALU.mult),
                        lambda e: e.tensor_tensor(st_wrk[:, 2:4], st_mu0[:, sl], st_mu0[:, sl], ALU.mult),
                        lambda e: e.tensor_tensor(st_wrk[:, 0:2], st_wrk[:, 0:2], st_wrk[:, 2:4], ALU.subtract),
                        lambda e: e.tensor_scalar(st_wrk[:, 4:6], st_wrk[:, 0:2], 1e-5, None, op0=ALU.add))
                m1 = sch.add("v", sm1, deps=[sumx_s[tc - 1], sumx_s[tc],
                                             sumq_s[tc - 1], sumq_s[tc]])
                if "dbgst" in debug_names and b == 0 and tc == 1:
                    dbg_probe_sts.append(sch.add("v", lambda e: e.tensor_copy(
                        dbg_sb[:, 12:14], st_sumq[:, 0:2]), deps=[sumq_s[1]]))
                    dbg_probe_sts.append(sch.add("s", lambda e: e.activation(
                        dbg_sb[:, 14:16], st_sumq[:, 0:2], AF.Identity),
                        deps=[sumq_s[1]]))
                    dbg_probe_sts.append(sch.add("v", lambda e: e.tensor_copy(
                        dbg_sb[:, 288:290], st_sumx[:, 0:2]), deps=[sumx_s[1]]))
                    dbg_probe_sts.append(sch.add("v", lambda e: e.tensor_copy(
                        dbg_sb[:, 0:8], st_wrk[:, 0:8]), deps=[m1]))
                s1 = sch.add("s", lambda e: e.activation(st_wrk[:, 6:8], st_wrk[:, 4:6], AF.Sqrt),
                             deps=[m1])
                if "dbgst" in debug_names and b == 0 and tc == 1:
                    dbg_probe_sts.append(sch.add("s", lambda e: e.activation(
                        dbg_sb[:, 8:10], st_wrk[:, 6:8], AF.Identity), deps=[s1]))
                def sm2(e, sl=sl):
                    return CH(e,
                        lambda e: e.reciprocal(st_r0[:, sl], st_wrk[:, 6:8]),
                        lambda e: e.tensor_tensor(st_wrk[:, 8:10], st_r0[:, sl], st_r0[:, sl], ALU.mult),
                        lambda e: e.tensor_tensor(st_wrk[:, 8:10], st_wrk[:, 8:10], st_wrk[:, 4:6], ALU.mult),
                        lambda e: e.tensor_scalar(st_wrk[:, 8:10], st_wrk[:, 8:10], -0.5, 1.5, op0=ALU.mult, op1=ALU.add),
                        lambda e: e.tensor_tensor(st_r0[:, sl], st_r0[:, sl], st_wrk[:, 8:10], ALU.mult),
                        lambda e: e.tensor_tensor(st_wrk[:, 10:12], st_r0[:, sl], st_r0[:, sl], ALU.mult),
                        lambda e: e.tensor_tensor(st_wrk[:, 10:12], st_wrk[:, 10:12], st_wrk[:, 0:2], ALU.mult),
                        lambda e: e.tensor_scalar(st_wrk[:, 12:14], st_wrk[:, 10:12], 1e-5, None, op0=ALU.add))
                m2 = sch.add("v", sm2, deps=[s1])
                if "dbgst" in debug_names and b == 0 and tc == 1:
                    dbg_probe_sts.append(sch.add("v", lambda e, sl=sl: e.tensor_copy(
                        dbg_sb[:, 10:12], st_r0[:, sl]), deps=[m2]))
                s2 = sch.add("s", lambda e: e.activation(st_wrk[:, 14:16], st_wrk[:, 12:14], AF.Sqrt),
                             deps=[m2])
                def sm3(e, sl=sl):
                    return CH(e,
                        lambda e: e.reciprocal(st_wrk[:, 16:18], st_wrk[:, 14:16]),
                        lambda e: e.tensor_tensor(st_wrk[:, 18:20], st_wrk[:, 16:18], st_wrk[:, 16:18], ALU.mult),
                        lambda e: e.tensor_tensor(st_wrk[:, 18:20], st_wrk[:, 18:20], st_wrk[:, 12:14], ALU.mult),
                        lambda e: e.tensor_scalar(st_wrk[:, 18:20], st_wrk[:, 18:20], -0.5, 1.5, op0=ALU.mult, op1=ALU.add),
                        lambda e: e.tensor_tensor(st_wrk[:, 16:18], st_wrk[:, 16:18], st_wrk[:, 18:20], ALU.mult),
                        lambda e: e.tensor_tensor(st_alp[:, sl], st_r0[:, sl], st_wrk[:, 16:18], ALU.mult),
                        lambda e: e.tensor_tensor(st_nma[:, sl], st_mu0[:, sl], st_alp[:, sl], ALU.mult),
                        lambda e: e.tensor_scalar(st_nma[:, sl], st_nma[:, sl], -1.0, None, op0=ALU.mult))
                m3 = sch.add("v", sm3, deps=[s2])
                if "dbgst" in debug_names and b == 0 and tc == 1:
                    dbg_probe_sts.append(sch.add("v", lambda e: e.tensor_copy(
                        dbg_sb[:, 290:292], st_sumq[:, 0:2]), deps=[m3]))
                stat_done[tc - 1] = stat_done[tc] = m3
                emit_tail(tc - 1)
                emit_tail(tc)

        passa_tail = [u_done[15], ufm_ev[15], rt_ev[15], rt_ev[14], rt_ev[13],
                      xfm_ev[15], xz_ev[-1], xz_ev[-2], dsc_steps[-1], sumq_s[15]]
        dump(f"mu0_{b}", lambda b=b: st_mu0[:, b * 16:(b + 1) * 16], [128, 16], F32,
             [stat_done[15]])
        dump(f"sumx_{b}", lambda b=b: st_sumx[:, b * 16:(b + 1) * 16], [128, 16], F32,
             [stat_done[15]])
        dump(f"xt15_{b}", lambda: arf(A_XT[1], 128, 1024), [128, 1024], F32,
             [xt_dma[15]])
        dump(f"sumq_{b}", lambda b=b: st_sumq[:, b * 16:(b + 1) * 16], [128, 16], F32,
             [stat_done[15]])
        dump(f"r0_{b}", lambda b=b: st_r0[:, b * 16:(b + 1) * 16], [128, 16], F32,
             [stat_done[15]])
        dump(f"alp_{b}", lambda b=b: st_alp[:, b * 16:(b + 1) * 16], [128, 16], F32,
             [stat_done[15]])
        dump(f"ufm_{b}", lambda: arb(A_UFM, 128, 4096), [128, 4096], F32,
             [ufm_ev[15]])
        dump(f"xproj_{b}", lambda: bass.AP(art, P_XP, [[ARW, 128], [1, 2048]]),
             [128, 2048], F32, [dsc_steps[-1]])
        dump(f"zgc_{b}", lambda: bass.AP(art, P_ZG, [[ARW, 128], [1, 2048]]),
             [128, 2048], F32, [xz_ev[-1], dsc_steps[-1]])
        dump(f"zg_{b}", lambda: bass.AP(art, P_ZG, [[ARW, 128], [1, 2048]]),
             [128, 2048], F32, [xz_ev[-1], dsc_steps[-1]])
        dump(f"brev_{b}", lambda: bass.AP(art, A_BREV, [[ARW, 16], [1, 2048]]),
             [16, 2048], F32, [brev_ev[-1]])

        def bttp(e):
            last = None
            for jp in range(16):
                src = bass.AP(art, A_BREV + jp * 128, [[ARW, 16], [1, 128]])
                j = 15 - jp
                last = e.transpose(
                    bass.AP(PS4[:, :].tensor, j * 16, [[1024, 128], [1, 16]]),
                    src, cid_bf[0:16, 0:16])
            return last
        t_bt = sch.add("t", bttp, deps=brev_ev + [ps4_last])
        v_bt = sch.add("v", lambda e: e.tensor_copy(
            bass.AP(BT[:, :].tensor, 0, [[256, 128], [1, 16], [16, 16]]),
            bass.AP(PS4[:, :].tensor, 0, [[1024, 128], [1, 256]])), deps=[t_bt])
        ps4_last = v_bt
        dump(f"bt_{b}", lambda: BT[:], [128, 256], F32, [v_bt])

        ld_xw = sch.add("g", lambda e, b=b: e.dma_start(
            rk_w[:, b * 16:(b + 1) * 16],
            bass.AP(rowa_h, b * 2048, [[1, 128], [128, 16]])),
            deps=[rt_ev[i] for i in range(16)], dma=True, q="gr")
        dump(f"xwraw{b}", lambda b=b: rk_w[:, b * 16:(b + 1) * 16], [128, 16], F32, [ld_xw])
        sl = slice(b * 16, (b + 1) * 16)
        def wfix(e, sl=sl):
            return CH(e,
                lambda e: e.scalar_tensor_tensor(rk_w[:, sl], st_mu0[:, sl], v_sm[:, 2:3],
                                                 rk_w[:, sl], op0=ALU.mult, op1=ALU.subtract),
                lambda e: e.tensor_tensor(rk_w[:, sl], rk_w[:, sl], st_r0[:, sl], ALU.mult),
                lambda e: e.tensor_scalar(rk_w[:, sl], rk_w[:, sl], -1.0, v_sm[:, 3:4],
                                          op0=ALU.mult, op1=ALU.add))
        v_fx = sch.add("v", wfix, deps=[ld_xw, v_rwsb, v_rbb, stat_done[15]])
        dump(f"logits{b}", lambda b=b: rk_w[:, b * 16:(b + 1) * 16], [128, 16], F32, [v_fx])
        d_wc = sch.add("g", lambda e, b=b: e.dma_start(
            bass.AP(rowa_h, (2 + b) * 4096, [[1, 128], [128, 16]]),
            rk_w[:, b * 16:(b + 1) * 16]), deps=[v_fx], dma=True, q="gr")
        wbc_ev = []
        for piece in range(4):
            ldp = sch.add("g", lambda e, piece=piece, b=b: e.dma_start(
                lrow[:], bass.AP(rowa_h, (2 + b) * 4096 + piece * 512, [[1, 1], [1, 512]])),
                deps=[d_wc, logrow_last], dma=True, q="gr")
            logrow_last = ldp
            bank = [PS0, PS1, PS2, PS3][piece]
            t_b2 = sch.add("t", lambda e, bank=bank: e.matmul(
                bank[:, :], ONES_R(), lrow[:], start=True, stop=True),
                deps=[ldp] + passa_tail + [v_bt] +
                     (prev_b["ps01_free"] if b else []))
            v_e = sch.add("v", lambda e, piece=piece, bank=bank: e.tensor_copy(
                arf(R_WBC + piece * 1024, 128, 512), bank[:, :]), deps=[t_b2] + passa_tail)
            wbc_ev.append(v_e)
            logrow_last = v_e
        g_prev = [None, None]
        rank_last = None
        rank_mms = []
        for jc in range(16):
            gap = arb(R_GB[jc % 2], 128, 2048)
            dep_g = wbc_ev + [v_fx, g_prev[jc % 2]]
            eng = "v"
            v_g = sch.add(eng, lambda e, gap=gap, jc=jc, b=b: e.tensor_scalar(
                gap, arf(R_WBC, 128, 2048), rk_w[:, b * 16 + jc:b * 16 + jc + 1],
                None, op0=ALU.is_gt), deps=dep_g)
            def rmm2(e, gap=gap, jc=jc):
                last = None
                for piece in range(4):
                    bank = [PS0, PS1, PS2, PS3][piece]
                    rhs = bass.AP(gap.tensor, gap.offset + piece * 512, [[ARW, 128], [1, 512]])
                    last = e.matmul(bank[0:1, :], cones_cb[:], rhs,
                                    start=(jc == 0), stop=(jc == 15))
                return last
            dep_r = [v_g, lc["ones_cb"], rank_last]
            if jc == 0:
                dep_r += wbc_ev
            rank_last = sch.add("t", rmm2, deps=dep_r)
            rank_mms.append(rank_last)
            g_prev[jc % 2] = rank_last
        rk_sts = []
        for piece in range(4):
            bank = [PS0, PS1, PS2, PS3][piece]
            v_e = sch.add("v", lambda e, bank=bank: e.tensor_copy(lrow[:], bank[0:1, :]),
                          deps=[rank_last, logrow_last])
            if piece < 2:
                ps01_last[piece] = v_e
            d_s = sch.add("g", lambda e, b=b, piece=piece: e.dma_start(
                bass.AP(rowa_h, (4 + b) * 4096 + piece * 512, [[1, 1], [1, 512]]), lrow[:]),
                deps=[v_e], dma=True, q="gr")
            logrow_last = d_s
            rk_sts.append(d_s)
        ld_rk = sch.add("g", lambda e, b=b: e.dma_start(
            rk_rank[:, b * 16:(b + 1) * 16],
            bass.AP(rowa_h, (4 + b) * 4096, [[1, 128], [128, 16]])), deps=rk_sts,
            dma=True, q="gr")
        v_sel = sch.add("v", lambda e, b=b: e.tensor_scalar(
            rk_sel[:, b * 16:(b + 1) * 16], rk_rank[:, b * 16:(b + 1) * 16], 1792.5, None,
            op0=ALU.is_gt), deps=[ld_rk])
        t_cs = sch.add("t", lambda e, b=b: e.matmul(
            PS3[:, 0:16], ctri_f[:], rk_sel[:, b * 16:(b + 1) * 16], start=True, stop=True),
            deps=[v_sel, lc["tri_f"]])
        t_tot = sch.add("t", lambda e, b=b: e.matmul(
            PS2[0:1, 32:48], ONES_C(), rk_sel[:, b * 16:(b + 1) * 16],
            start=True, stop=True), deps=[t_cs])
        def offs(e):
            return CH(e,
                lambda e: e.tensor_copy(st_wrk[0:1, 32:48], PS2[0:1, 32:48]),
                lambda e: e.tensor_tensor_scan(st_wrk[0:1, 48:64], st_wrk[0:1, 32:48],
                                               ctri_f[0:1, 0:16], 0.0, op0=ALU.add, op1=ALU.mult),
                lambda e: e.tensor_tensor(st_wrk[0:1, 32:48], st_wrk[0:1, 48:64],
                                          st_wrk[0:1, 32:48], ALU.subtract))
        v_of = sch.add("v", offs, deps=[t_cs, t_tot, lc["tri_f"]])
        t_ob = sch.add("t", lambda e: e.matmul(PS2[:, 0:16], ONES_R(), st_wrk[0:1, 32:48],
                                               start=True, stop=True), deps=[v_of])
        def omf(e, b=b):
            sl = slice(b * 16, (b + 1) * 16)
            cjk = cjk0 if b == 0 else cjk1
            return CH(e,
                lambda e: e.tensor_copy(st_wrk[:, 32:48], PS2[:, 0:16]),
                lambda e: e.tensor_tensor(rk_csum[:, sl], PS3[:, 0:16], st_wrk[:, 32:48], ALU.add),
                lambda e: e.tensor_scalar(rk_omf[:, sl], rk_csum[:, sl], -256.0, None, op0=ALU.add),
                lambda e: e.tensor_tensor(rk_omf[:, sl], rk_omf[:, sl], rk_sel[:, sl], ALU.mult),
                lambda e: e.tensor_scalar(rk_omf[:, sl], rk_omf[:, sl], 255.0 + 256.0 * b, None, op0=ALU.add),
                lambda e: e.tensor_scalar(st_wrk[:, 32:48], rk_sel[:, sl], -1.0, 1.0,
                                          op0=ALU.mult, op1=ALU.add),
                lambda e: e.tensor_tensor(st_wrk[:, 32:48], st_wrk[:, 32:48], cjk[:], ALU.mult),
                lambda e: e.tensor_tensor(rk_omf[:, sl], rk_omf[:, sl], st_wrk[:, 32:48], ALU.add),
                lambda e: e.tensor_scalar(rk_omf[:, sl], rk_omf[:, sl], 1023.0, None, op0=ALU.min),
                lambda e: e.tensor_copy(rk_omi[:, sl], rk_omf[:, sl]))
        v_om = sch.add("v", omf, deps=[t_ob, lc["cjk0"], lc["cjk1"]])
        d_z = sch.add("g", lambda e, b=b: e.dma_start(
            bass.AP(possel_h, b * 256 + 255, [[1, 1], [1, 1]]), czero_i[:]),
            deps=[lc["zi"]], dma=True, q="gr")
        sc_sts = []
        for c in range(16):
            def psc(e, b=b, c=c):
                pay = ciota if b == 0 else ciota2
                return e.indirect_dma_start(
                    bass.AP(possel_h, 0, [[1, 1024], [1, 1]]),
                    IndirectOffsetOnAxis(rk_omi[:, b * 16 + c:b * 16 + c + 1], 0),
                    pay[:, c:c + 1], None)
            sc_sts.append(sch.add("g", psc, deps=[v_om, d_z, z_ps, lc["iota"], lc["iota2"]],
                                  dma=True, q=f"gs{c % 4}"))
        def sgw(e, b=b):
            e.memset(bass.AP(rk_sg[:, :].tensor, 3, [[64, 128], [4, 16]]), 0.0)
            e.tensor_copy(bass.AP(rk_sg[:, :].tensor, 0, [[64, 128], [4, 16]]),
                          rk_w[:, b * 16:(b + 1) * 16])
            e.tensor_copy(bass.AP(rk_sg[:, :].tensor, 1, [[64, 128], [4, 16]]),
                          st_mu0[:, b * 16:(b + 1) * 16])
            return e.tensor_copy(bass.AP(rk_sg[:, :].tensor, 2, [[64, 128], [4, 16]]),
                                 st_r0[:, b * 16:(b + 1) * 16])
        v_sg = sch.add("v", sgw, deps=[v_fx, v_om])
        d_sg = sch.add("g", lambda e, b=b: e.dma_start(
            bass.AP(sg_h, b * 2048 * 4, [[4, 128], [512, 16], [1, 4]]), rk_sg[:]),
            deps=[v_sg, z_sg], dma=True, q="gr")
        ld_ps = sch.add("g", lambda e, b=b: e.dma_start(
            sp_idx[:, b * 2:(b + 1) * 2],
            bass.AP(possel_h, b * 256, [[1, 128], [128, 2]])), deps=sc_sts,
            dma=True, q="gr")
        def betaf(e, b=b):
            sl = slice(b * 16, (b + 1) * 16)
            return CH(e,
                lambda e: e.tensor_scalar(st_wrk[:, 48:64], rk_sel[:, sl], -1.0, 2.0, op0=ALU.mult, op1=ALU.add),
                lambda e: e.tensor_tensor(st_wrk[:, 48:64], st_wrk[:, 48:64], st_r0[:, sl], ALU.mult),
                lambda e: e.tensor_scalar(rk_b1[:, sl], st_wrk[:, 48:64], 1.0, None, op0=ALU.add),
                lambda e: e.tensor_tensor(rk_b2[:, sl], st_wrk[:, 48:64], st_mu0[:, sl], ALU.mult),
                lambda e: e.tensor_scalar(rk_b2[:, sl], rk_b2[:, sl], -1.0, None, op0=ALU.mult))
        v_beta = sch.add("v", betaf, deps=[v_sel, v_om])
        dump(f"sel{b}", lambda b=b: rk_sel[:, b * 16:(b + 1) * 16], [128, 16], F32, [v_sel])
        dump(f"om{b}", lambda b=b: rk_omf[:, b * 16:(b + 1) * 16], [128, 16], F32, [v_om])

        conv_last = None
        h_prev = [None, None]
        for s_i in range(S):
            hbuf = C_HREV[s_i % 2]
            dep_h = [apad_done[s_i], h_prev[s_i % 2]]
            if b == 0 and s_i < 2:
                dep_h += [xz_ev[-1]]
            if b == 1 and s_i < 2:
                dep_h += prev_b["c_done"]
            d_h = sch.add("g", lambda e, s_i=s_i, hbuf=hbuf: e.dma_start(
                arb(hbuf, 128, 2048), bass.AP(apad_h, s_i * 2176 + 1, [[1, 128], [1, 2048]])),
                deps=dep_h, dma=True, q="gh")
            def conv_s(e, s_i=s_i, hbuf=hbuf):
                last = None
                for dd in range(16):
                    lhs = arb(hbuf + 128 * dd, 128, 128)
                    n = 16 - dd
                    rhs = bass.AP(BT[:, :].tensor, s_i * 16, [[256, 128], [1, n]])
                    outp = bass.AP(PS0[:, :].tensor, s_i * 16 + dd, [[512, 128], [1, n]])
                    last = e.matmul(outp, lhs, rhs,
                                    start=(s_i == 0 and dd == 0),
                                    stop=(s_i == S - 1 and dd == 15))
                return last
            dep_c = [d_h, conv_last, v_bt]
            if s_i == 0:
                dep_c += [ld_rk]
            conv_last = sch.add("t", conv_s, deps=dep_c)
            h_prev[s_i % 2] = conv_last
        v_xm = sch.add("v", lambda e: e.tensor_copy(
            bass.AP(XMS[:, :].tensor, 0, [[256, 128], [1, 16], [16, 16]]),
            PS0[:, 0:256]), deps=[conv_last])
        dump(f"xm{b}", lambda b=b: XMS[:], [128, 256], F32, [v_xm])
        xmt_evs = []
        t_prevx = None
        for rnd in range(2):
            def xmtp(e, rnd=rnd):
                last = None
                for i in range(8):
                    g = rnd * 8 + i
                    last = e.transpose(PS4[0:16, i * 128:(i + 1) * 128],
                                       XMS[:, g * 16:(g + 1) * 16], cid_bf[:])
                return last
            t_x2 = sch.add("t", xmtp, deps=[v_xm, ps4_last, t_prevx] + xmt_evs[-1:])
            v_x2 = sch.add("v", lambda e, rnd=rnd: e.tensor_copy(
                bass.AP(XMT[:, :].tensor, rnd * 1024, [[2048, 16], [1, 1024]]),
                bass.AP(PS4[:, :].tensor, 0, [[1024, 16], [1, 1024]])), deps=[t_x2])
            xmt_evs.append(v_x2)
            t_prevx = t_x2
        v_xmt = xmt_evs[-1]
        ps4_last = v_xmt

        gdeps = [ld_ps, d_sg, z_misc]
        if b:
            gdeps += prev_b["b_region_free"]
        gsts = []
        gq = ["gg", "gs0", "gs1", "gs2", "gs3", "gg"]
        for rc in range(2):
            nrow = 128 if rc == 0 else 127
            idx = sp_idx[0:nrow, b * 2 + rc:b * 2 + rc + 1]
            dst = bass.AP(art, B_XSEL + rc * 2048, [[ARW, nrow], [1, 2048]]).bitcast(F32)
            gsts.append(sch.add("g", lambda e, idx=idx, dst=dst: e.indirect_dma_start(
                dst, None, bass.AP(x_h, 0, [[1024, 4096], [1, 1024]]),
                IndirectOffsetOnAxis(idx, 0)), deps=gdeps, dma=True, q=gq[rc * 3]))
            gsts.append(sch.add("g", lambda e, idx=idx, rc=rc, nrow=nrow: e.indirect_dma_start(
                sp_sg[0:nrow, rc * 4:(rc + 1) * 4], None,
                bass.AP(sg_h, 0, [[4, 4096], [1, 4]]),
                IndirectOffsetOnAxis(idx, 0)), deps=gdeps, dma=True, q=gq[rc * 3 + 1]))
            gsts.append(sch.add("g", lambda e, idx=idx, rc=rc, b=b, nrow=nrow: e.indirect_dma_start(
                sp_posi[0:nrow, b * 2 + rc:b * 2 + rc + 1], None,
                bass.AP(pos_h, 0, [[1, 4096], [1, 1]]),
                IndirectOffsetOnAxis(idx, 0)), deps=gdeps, dma=True, q=gq[rc * 3 + 2]))
        v_pc = sch.add("v", lambda e, b=b: e.tensor_copy(
            sp_posf[:, b * 2:(b + 1) * 2], sp_posi[:, b * 2:(b + 1) * 2]), deps=gsts)
        v_nm = sch.add("v", lambda e: e.tensor_scalar(
            sp_r1[:, 8:10], bass.AP(sp_sg[:, :].tensor, 1, [[16, 128], [4, 2]]),
            -1.0, None, op0=ALU.mult), deps=gsts)
        nsts = []
        for rc in range(2):
            xs = bass.AP(art, B_XSEL + rc * 2048, [[ARW, 128], [1, 2048]]).bitcast(F32)
            nsts.append(sch.add("v", lambda e, xs=xs, rc=rc: e.tensor_scalar(
                xs, xs, sp_r1[:, 8 + rc:9 + rc],
                bass.AP(sp_sg[:, :].tensor, rc * 4 + 2, [[16, 128], [1, 1]]),
                op0=ALU.add, op1=ALU.mult), deps=[v_nm]))
        dump(f"xnsel{b}", lambda b=b: bass.AP(art, B_XSEL, [[ARW, 128], [1, 2048]]).bitcast(F32),
             [128, 1024], F32, nsts)
        junk = arb(A_JUNK, 128, 1024)
        q1 = []
        for rc in range(2):
            xs = bass.AP(art, B_XSEL + rc * 2048, [[ARW, 128], [1, 2048]]).bitcast(F32)
            def q1_f(e, xs=xs, rc=rc, junk=junk):
                e.activation(junk, xs, AF.Square, accum_out=sp_r1[:, rc:rc + 1])
                return e.activation(vdum[:, 4:6], vdum[:, 2:4], AF.Identity)
            q1.append(sch.add("s", q1_f, deps=nsts + (q1[-1:] if q1 else [sumq_s[15]])))
        v_a1 = sch.add("v", lambda e: e.tensor_scalar(
            sp_r1[:, 0:2], sp_r1[:, 0:2], 1.0 / 1024.0, 1e-6, op0=ALU.mult, op1=ALU.add),
            deps=q1)
        s_a1 = sch.add("s", lambda e: e.activation(sp_r1[:, 2:4], sp_r1[:, 0:2], AF.Sqrt),
                       deps=[v_a1])
        def nwt(e):
            return CH(e,
                lambda e: e.reciprocal(sp_r1[:, 4:6], sp_r1[:, 2:4]),
                lambda e: e.tensor_tensor(sp_r1[:, 6:8], sp_r1[:, 4:6], sp_r1[:, 4:6], ALU.mult),
                lambda e: e.tensor_tensor(sp_r1[:, 6:8], sp_r1[:, 6:8], sp_r1[:, 0:2], ALU.mult),
                lambda e: e.tensor_scalar(sp_r1[:, 6:8], sp_r1[:, 6:8], -0.5, 1.5, op0=ALU.mult, op1=ALU.add),
                lambda e: e.tensor_tensor(sp_r1[:, 4:6], sp_r1[:, 4:6], sp_r1[:, 6:8], ALU.mult))
        v_n1 = sch.add("v", nwt, deps=[s_a1])
        hsts = []
        for rc in range(2):
            xs = bass.AP(art, B_XSEL + rc * 2048, [[ARW, 128], [1, 2048]]).bitcast(F32)
            hnap = arb(B_HN + rc * 1024, 128, 1024)
            hsts.append(sch.add("v", lambda e, xs=xs, hnap=hnap, rc=rc: e.tensor_scalar(
                hnap, xs, sp_r1[:, 4 + rc:5 + rc], None, op0=ALU.mult), deps=[v_n1]))
        hnf = []
        for dc in range(8):
            def htp(e, dc=dc):
                last = None
                for rc in range(2):
                    src = arb(B_HN + rc * 1024 + dc * 128, 128, 128)
                    last = e.transpose(PS4[:, rc * 128:(rc + 1) * 128], src, cid_bf[:])
                return last
            t_h = sch.add("t", htp, deps=hsts + [ps4_last])
            v_h = sch.add("s", lambda e, dc=dc: e.activation(
                arb(B_HNF + dc * 256, 128, 256),
                bass.AP(PS4[:, :].tensor, 0, [[1024, 128], [1, 256]]),
                AF.Identity, scale=v_ln1[:, dc:dc + 1]), deps=[t_h, ld_ln1])
            hnf.append(v_h)
            ps4_last = v_h

        def load_w1024(h_t, off, deps):
            return sch.add("g", lambda e: e.dma_start(
                bass.AP(wart, off, [[16384, 128], [1024, 8], [1, 1024]]),
                bass.AP(h_t, 0, [[1024, 128], [128 * 1024, 8], [1, 1024]])),
                deps=deps, dma=True, q="gw")

        war_free = xz_ev[-4:] + dsc_steps[-1:] + [t for t in (rank_mms[-1],)]
        ld_q = load_w1024(wq_h, 0, war_free + (prev_b["war_free"] if b else []))
        ld_k = load_w1024(wk_h, 8192, [ld_q])

        def proj(dst_off, war_off, scale_step, dep_w):
            evs = []
            for ec in range(8):
                bank, bi = (PS0, 0) if ec % 2 == 0 else (PS1, 1)
                def pmm(e, ec=ec, war_off=war_off, bank=bank):
                    last = None
                    for c in range(8):
                        lhs = bass.AP(wart, war_off + c * 1024 + ec * 128,
                                      [[16384, 128], [1, 128]])
                        rhs = arb(B_HNF + c * 256, 128, 255)
                        last = e.matmul(bank[:, 0:255], lhs, rhs, start=(c == 0), stop=(c == 7))
                    return last
                dep_p = dep_w + hnf + [ps01_last[bi]]
                if dst_off == B_QF and ec < 2:
                    dep_p += [v_xm, conv_last]
                t_p = sch.add("t", pmm, deps=dep_p)
                if scale_step:
                    v_p = sch.add("s", lambda e, ec=ec, bank=bank, dst_off=dst_off: e.activation(
                        arb(dst_off + ec * 256, 128, 255), bank[:, 0:255],
                        AF.Identity, scale=cq8[:]), deps=[t_p, lc["q8"]])
                else:
                    v_p = sch.add("v", lambda e, ec=ec, bank=bank, dst_off=dst_off: e.tensor_copy(
                        arb(dst_off + ec * 256, 128, 255), bank[:, 0:255]), deps=[t_p])
                ps01_last[bi] = v_p
                evs.append(v_p)
            return evs
        q_evs = proj(B_QF, 0, True, [ld_q])
        k_evs = proj(B_KF, 8192, False, [ld_k])
        ld_v = load_w1024(wv_h, 0, q_evs)
        v_evs = proj(B_VF, 0, False, [ld_v])

        rope_tab = []
        for rc in range(2):
            ang = arf(B_ANG, 128, 128)
            ang2 = arf(B_ANG + 256, 128, 128)
            ki = arf(B_SCR, 128, 128).bitcast(I32)
            kf = arf(B_SCR + 256, 128, 128)
            def angf(e, rc=rc, b=b, ang=ang, ang2=ang2, ki=ki, kf=kf):
                return CH(e,
                    lambda e: e.tensor_scalar(ang, cinv2[:], sp_posf[:, b * 2 + rc:b * 2 + rc + 1],
                                              None, op0=ALU.mult),
                    lambda e: e.tensor_scalar(ang2, ang, INV2PI, None, op0=ALU.mult),
                    lambda e: e.tensor_copy(ki, ang2),
                    lambda e: e.tensor_copy(kf, ki),
                    lambda e: e.scalar_tensor_tensor(ang, kf, N2PI, ang, op0=ALU.mult, op1=ALU.add),
                    lambda e: e.tensor_scalar(kf, ang, PI, None, op0=ALU.is_gt),
                    lambda e: e.scalar_tensor_tensor(ang, kf, N2PI, ang, op0=ALU.mult, op1=ALU.add),
                    lambda e: e.tensor_scalar(ang2, ang, PI / 2.0, None, op0=ALU.add),
                    lambda e: e.tensor_scalar(kf, ang2, PI, None, op0=ALU.is_gt),
                    lambda e: e.scalar_tensor_tensor(ang2, kf, N2PI, ang2, op0=ALU.mult, op1=ALU.add))
            v_an = sch.add("v", angf, deps=[v_pc, lc["inv2"]] + rope_tab)
            sin_t = arb(B_SCR, 128, 128)
            cos_t = arb(B_SCR + 128, 128, 128)
            s_s = sch.add("s", lambda e, ang=ang, sin_t=sin_t: e.activation(
                sin_t, ang, AF.Sin), deps=[v_an])
            s_c = sch.add("s", lambda e, ang2=ang2, cos_t=cos_t: e.activation(
                cos_t, ang2, AF.Sin), deps=[v_an, s_s])
            def rtp(e, sin_t=sin_t, cos_t=cos_t):
                e.transpose(PS4[:, 0:128], sin_t, cid_bf[:])
                return e.transpose(PS4[:, 128:256], cos_t, cid_bf[:])
            t_r = sch.add("t", rtp, deps=[s_s, s_c, ps4_last])
            def rev_(e, rc=rc):
                e.tensor_copy(arb(B_SIN + rc * 128, 128, 128),
                              bass.AP(PS4[:, :].tensor, 0, [[1024, 128], [1, 128]]))
                return e.tensor_copy(arb(B_COS + rc * 128, 128, 128),
                                     bass.AP(PS4[:, :].tensor, 128, [[1024, 128], [1, 128]]))
            v_r = sch.add("v", rev_, deps=[t_r])
            rope_tab.append(v_r)
            ps4_last = v_r
        rope_done = []
        ps2_free = [v_rbb]
        for (off, evs) in ((B_QF, q_evs), (B_KF, k_evs)):
            for ec in range(8):
                def rotmm(e, off=off, ec=ec):
                    return e.matmul(PS2[:, 0:255], crot[:], arb(off + ec * 256, 128, 255),
                                    start=True, stop=True)
                dep_rm = [evs[ec], lc["rot"]] + rope_tab + rope_done[-1:]
                if not rope_done:
                    dep_rm += [v_om]
                t_rm = sch.add("t", rotmm, deps=dep_rm)
                def rap(e, off=off, ec=ec):
                    tcos = arb(B_SCR, 128, 255)
                    tsin = arb(B_SCR + 256, 128, 255)
                    return CH(e,
                        lambda e: e.tensor_tensor(tcos, arb(off + ec * 256, 128, 255),
                                                  arb(B_COS, 128, 255), ALU.mult),
                        lambda e: e.tensor_tensor(tsin, PS2[:, 0:255], arb(B_SIN, 128, 255), ALU.mult),
                        lambda e: e.tensor_tensor(arb(off + ec * 256, 128, 255), tcos, tsin, ALU.add))
                v_ra = sch.add("v", rap, deps=[t_rm])
                rope_done.append(v_ra)
        o_done = []
        sc_evs = []
        for hh in range(16):
            ec = hh // 2
            po = (hh % 2) * 64
            sc_evs = []
            for qc in range(2):
                nq = 128 if qc == 0 else 127
                def smm2(e, ec=ec, po=po, qc=qc, nq=nq):
                    lhs = bass.AP(art, po * ARW + B_QF + ec * 256 + qc * 128,
                                  [[ARW, 64], [1, nq]])
                    rhs = bass.AP(art, po * ARW + B_KF + ec * 256, [[ARW, 64], [1, 255]])
                    return e.matmul(PS3[0:nq, 0:255], lhs, rhs, start=True, stop=True)
                dep_s = rope_done + (o_done[-1:] if o_done else []) + sc_evs[-1:]
                if hh == 0 and qc == 0:
                    dep_s += [v_om, v_of]
                t_sc = sch.add("t", smm2, deps=dep_s)
                nmx = sp_r1[:, 10:11]
                v_mx = sch.add("v", lambda e, nq=nq, nmx=nmx: e.reduce_max(
                    nmx[0:nq, :], PS3[0:nq, 0:255], axis=AX.X, negate=True), deps=[t_sc])
                eb = bass.AP(art, B_EB + qc * 256, [[ARW, nq], [1, 255]])
                s_ex = sch.add("s", lambda e, eb=eb, nq=nq, nmx=nmx: e.activation(
                    eb, PS3[0:nq, 0:255], AF.Exp, bias=nmx[0:nq, :]), deps=[v_mx])
                trit = ctri0 if qc == 0 else ctri1
                g_tz = sch.add("g", lambda e, eb=eb, nq=nq, trit=trit: e.tensor_tensor(
                    eb, eb, trit[0:nq, 0:255], ALU.mult),
                    deps=[s_ex, lc["tri0"], lc["tri1"]])
                smv = sp_r1[:, 11:12]
                v_s2 = sch.add("v", lambda e, eb=eb, nq=nq, smv=smv: e.reduce_sum(
                    smv[0:nq, :], eb, axis=AX.X), deps=[g_tz])
                def nrm(e, eb=eb, nq=nq, smv=smv):
                    return CH(e,
                        lambda e: e.reciprocal(sp_r1[0:nq, 12:13], smv[0:nq, :]),
                        lambda e: e.tensor_scalar(eb, eb, sp_r1[0:nq, 12:13], None, op0=ALU.mult))
                v_nr = sch.add("v", nrm, deps=[v_s2])
                sc_evs.append(v_nr)
            def atp(e):
                last = None
                for qc in range(2):
                    nq = 128 if qc == 0 else 127
                    src0 = bass.AP(art, B_EB + qc * 256, [[ARW, nq], [1, 128]])
                    last = e.transpose(PS4[0:128, qc * 128:qc * 128 + nq], src0,
                                       cid_bf[0:nq, 0:nq])
                    src1 = bass.AP(art, B_EB + qc * 256 + 128, [[ARW, nq], [1, 127]])
                    last = e.transpose(PS4[0:127, 512 + qc * 128:512 + qc * 128 + nq], src1,
                                       cid_bf[0:nq, 0:nq])
                return last
            t_at = sch.add("t", atp, deps=sc_evs + [ps4_last])
            def atev(e):
                e.tensor_copy(arb(B_AT, 128, 255),
                              bass.AP(PS4[:, :].tensor, 0, [[1024, 128], [1, 255]]))
                e.tensor_copy(arb(B_AT + 256, 127, 255),
                              bass.AP(PS4[:, :].tensor, 512, [[1024, 127], [1, 255]]))
                return e.tensor_scalar(vdum[:, 0:2], vdum[:, 2:4], 1.0, None, op0=ALU.mult)
            v_at = sch.add("v", atev, deps=[t_at])
            ps4_last = v_at
            def vtp(e, ec=ec, po=po):
                src0 = bass.AP(art, po * ARW + B_VF + ec * 256, [[ARW, 64], [1, 128]])
                e.transpose(PS4[0:128, 768:832], src0, cid_bf[po:po + 64, po:po + 64])
                src1 = bass.AP(art, po * ARW + B_VF + ec * 256 + 128, [[ARW, 64], [1, 127]])
                return e.transpose(PS4[0:127, 832:896], src1, cid_bf[po:po + 64, po:po + 64])
            t_vt = sch.add("t", vtp, deps=v_evs + [v_at])
            def vtev(e):
                e.tensor_copy(arb(B_VT, 128, 64),
                              bass.AP(PS4[:, :].tensor, 768, [[1024, 128], [1, 64]]))
                e.tensor_copy(arb(B_VT + 64, 127, 64),
                              bass.AP(PS4[:, :].tensor, 832, [[1024, 127], [1, 64]]))
                return e.tensor_scalar(vdum[:, 0:2], vdum[:, 2:4], 1.0, None, op0=ALU.mult)
            v_vt = sch.add("v", vtev, deps=[t_vt])
            ps4_last = v_vt
            def omm(e, po=po):
                e.matmul(PS2[po:po + 64, 0:255], arb(B_VT, 128, 64), arb(B_AT, 128, 255),
                         start=True, stop=False)
                return e.matmul(PS2[po:po + 64, 0:255], arb(B_VT + 64, 127, 64),
                                arb(B_AT + 256, 127, 255), start=False, stop=True)
            dep_o = [v_at, v_vt] + rope_done[-1:]
            t_o = sch.add("t", omm, deps=dep_o)
            def oev(e, ec=ec, po=po):
                e.tensor_copy(
                    bass.AP(art, po * ARW + B_KF + ec * 256, [[ARW, 64], [1, 255]]),
                    PS2[po:po + 64, 0:255])
                return e.tensor_scalar(vdum[:, 0:2], vdum[:, 2:4], 1.0, None, op0=ALU.mult)
            v_o = sch.add("v", oev, deps=[t_o] + k_evs + rope_done)
            o_done.append(v_o)
        dump(f"hnf_{b}", lambda: arb(B_HNF, 128, 2048), [128, 2048], F32, hnf)
        dump(f"qf_{b}", lambda: arb2(B_QF, 128, [[256, 8], [1, 255]]), [128, 2040], F32, rope_done)
        dump(f"kf_{b}", lambda: arb2(B_KF, 128, [[256, 8], [1, 255]]), [128, 2040], F32, rope_done)
        dump(f"eb_{b}", lambda: arb(B_EB, 128, 255), [128, 255], F32, sc_evs)
        dump(f"ofm_{b}", lambda: arb2(B_KF, 128, [[256, 8], [1, 255]]), [128, 2040], F32, o_done)
        ld_o = load_w1024(wo_h, 8192, o_done[:4])
        xo_evs = []
        for ec in range(8):
            bank, bi = (PS0, 0) if ec % 2 == 0 else (PS1, 1)
            def wmm2(e, ec=ec, bank=bank):
                last = None
                for c in range(8):
                    lhs = bass.AP(wart, 8192 + c * 1024 + ec * 128, [[16384, 128], [1, 128]])
                    rhs = arb(B_KF + c * 256, 128, 255)
                    last = e.matmul(bank[:, 0:255], lhs, rhs, start=(c == 0), stop=(c == 7))
                return last
            t_w = sch.add("t", wmm2, deps=[ld_o] + o_done + [ps01_last[bi]])
            v_w = sch.add("v", lambda e, ec=ec, bank=bank: e.tensor_copy(
                arb(B_QF + ec * 256, 128, 255), bank[:, 0:255]),
                deps=[t_w] + rope_done + q_evs)
            ps01_last[bi] = v_w
            xo_evs.append(v_w)
        dump(f"xofm_{b}", lambda: arb2(B_QF, 128, [[256, 8], [1, 255]]), [128, 2040], F32, xo_evs)
        x1_done = []
        for rc in range(2):
            nq = 128 if rc == 0 else 127
            def xotp(e, rc=rc, nq=nq):
                last = None
                for dc in range(8):
                    src = bass.AP(art, B_QF + dc * 256 + rc * 128, [[ARW, 128], [1, nq]])
                    last = e.transpose(PS4[0:nq, dc * 128:dc * 128 + 128], src, cid_bf[:])
                return last
            t_xo = sch.add("t", xotp, deps=xo_evs + [ps4_last])
            xs = bass.AP(art, B_XSEL + rc * 2048, [[ARW, nq], [1, 2048]]).bitcast(F32)
            x1ap = bass.AP(art, B_PROC + rc * 2048, [[ARW, nq], [1, 2048]]).bitcast(F32)
            v_x1 = sch.add("v", lambda e, xs=xs, x1ap=x1ap, nq=nq: e.tensor_add(
                x1ap, xs, bass.AP(PS4[:, :].tensor, 0, [[1024, nq], [1, 1024]])),
                deps=[t_xo] + hsts + (prev_b["proc_free"] if b and rc == 0 else []))
            x1_done.append(v_x1)
            ps4_last = v_x1
        dump(f"x1_{b}", lambda: bass.AP(art, B_PROC, [[ARW, 128], [1, 2048]]).bitcast(F32),
             [128, 1024], F32, x1_done)
        q2 = []
        for rc in range(2):
            xs = bass.AP(art, B_PROC + rc * 2048, [[ARW, 128], [1, 2048]]).bitcast(F32)
            def q2_f(e, xs=xs, junk=junk, rc=rc):
                e.activation(junk, xs, AF.Square, accum_out=sp_r1[:, rc:rc + 1])
                return e.activation(vdum[:, 4:6], vdum[:, 2:4], AF.Identity)
            q2.append(sch.add("s", q2_f, deps=x1_done + (q2[-1:] if q2 else [])))
        v_a2 = sch.add("v", lambda e: e.tensor_scalar(
            sp_r1[:, 0:2], sp_r1[:, 0:2], 1.0 / 1024.0, 1e-6, op0=ALU.mult, op1=ALU.add),
            deps=q2)
        s_a2 = sch.add("s", lambda e: e.activation(sp_r1[:, 2:4], sp_r1[:, 0:2], AF.Sqrt),
                       deps=[v_a2])
        v_n2 = sch.add("v", nwt, deps=[s_a2])
        x1n = []
        for rc in range(2):
            xs = bass.AP(art, B_PROC + rc * 2048, [[ARW, 128], [1, 2048]]).bitcast(F32)
            hnap = arb(B_HN + rc * 1024, 128, 1024)
            x1n.append(sch.add("v", lambda e, xs=xs, hnap=hnap, rc=rc: e.tensor_scalar(
                hnap, xs, sp_r1[:, 4 + rc:5 + rc], None, op0=ALU.mult),
                deps=[v_n2] + hnf))
        hnf2 = []
        for dc in range(8):
            def htp2(e, dc=dc):
                last = None
                for rc in range(2):
                    src = arb(B_HN + rc * 1024 + dc * 128, 128, 128)
                    last = e.transpose(PS4[:, rc * 128:(rc + 1) * 128], src, cid_bf[:])
                return last
            t_h = sch.add("t", htp2, deps=x1n + [ps4_last])
            def hnf2_f(e, dc=dc):
                dst = arb(B_HNF + dc * 256, 128, 256)
                e.activation(dst, bass.AP(PS4[:, :].tensor, 0, [[1024, 128], [1, 256]]),
                             AF.Identity, scale=v_ln2[:, dc:dc + 1])
                return e.activation(vdum[:, 4:6], vdum[:, 2:4], AF.Identity)
            v_h = sch.add("s", hnf2_f, deps=[t_h, ld_ln2] + hnf)
            hnf2.append(v_h)
            ps4_last = v_h
        dump(f"hnf2_{b}", lambda: arb(B_HNF, 128, 2048), [128, 2048], F32, hnf2)
        w1_slots = [0, 1024, 2048, 3072]
        w2_slots = [4096, 5120]
        gl_done = []
        x2_last = None
        ld_w1p = {}
        w2_mms = []
        for fc in range(32):
            sl1 = w1_slots[fc % 4]
            dep1 = ([ld_o] + xo_evs) if fc < 4 else [gl_done[fc - 4]]
            ld_w1p[fc] = sch.add("g", lambda e, fc=fc, sl1=sl1: e.dma_start(
                bass.AP(wart, sl1, [[16384, 128], [128, 8], [1, 128]]),
                bass.AP(w1_h, fc * 128, [[4096, 128], [128 * 4096, 8], [1, 128]])),
                deps=dep1, dma=True, q="gw")
            sl2 = w2_slots[fc % 2]
            dep2 = [ld_w1p[fc]] if fc < 2 else [ld_w1p[fc], gl_done[fc - 2],
                                               w2_mms[fc - 2]]
            ld_w2p = sch.add("g", lambda e, fc=fc, sl2=sl2: e.dma_start(
                bass.AP(wart, sl2, [[16384, 128], [1, 1024]]),
                bass.AP(w2_h, fc * 128 * 1024, [[1024, 128], [1, 1024]])),
                deps=dep2, dma=True, q="gw")
            bank, bi = (PS0, 0) if fc % 2 == 0 else (PS1, 1)
            def w1mm(e, sl1=sl1, bank=bank):
                last = None
                for c in range(8):
                    lhs = bass.AP(wart, sl1 + c * 128, [[16384, 128], [1, 128]])
                    rhs = arb(B_HNF + c * 256, 128, 255)
                    last = e.matmul(bank[:, 0:255], lhs, rhs, start=(c == 0), stop=(c == 7))
                return last
            t_1 = sch.add("t", w1mm, deps=[ld_w1p[fc]] + hnf2 + [ps01_last[bi]])
            h1ap = arb(B_H1 + (fc % 2) * 256, 128, 255)
            def gel_f(e, h1ap=h1ap, bank=bank, fc=fc):
                e.activation(h1ap, bank[:, 0:255], AF.Gelu_apprx_tanh)
                return e.activation(vdum[:, 4:6], vdum[:, 2:4], AF.Identity)
            s_g = sch.add("s", gel_f, deps=[t_1])
            gl_done.append(s_g)
            ps01_last[bi] = s_g
            def w2mm(e, fc=fc, sl2=sl2, h1ap=h1ap):
                last = None
                for rc in range(2):
                    nq = 128 if rc == 0 else 127
                    lhs = bass.AP(h1ap.tensor, h1ap.offset + rc * 128, [[ARW, 128], [1, nq]])
                    for half in range(2):
                        rhs = bass.AP(wart, sl2 + half * 512, [[16384, 128], [1, 512]])
                        if rc == 0:
                            outp = PS5[0:nq, half * 512:(half + 1) * 512]
                        else:
                            outp = (PS2 if half == 0 else PS3)[0:nq, :]
                        last = e.matmul(outp, lhs, rhs, start=(fc == 0), stop=(fc == 31))
                return last
            dep_w2 = [ld_w2p, s_g, x2_last]
            if fc == 0:
                dep_w2 += [ps5_last, o_done[-1], sc_evs[-1], v_om, xfm_ev[15]]
            x2_last = sch.add("t", w2mm, deps=dep_w2)
            w2_mms.append(x2_last)
        dump(f"h1_{b}", lambda: arb2(B_H1, 128, [[256, 2], [1, 255]]),
             [128, 510], F32, [gl_done[-1]])
        proc_done = []
        for rc in range(2):
            nq = 128 if rc == 0 else 127
            x1ap = bass.AP(art, B_PROC + rc * 2048, [[ARW, nq], [1, 2048]]).bitcast(F32)
            wcol = bass.AP(sp_sg[:, :].tensor, rc * 4, [[16, nq], [1, 1]])
            v_p1 = sch.add("v", lambda e, x1ap=x1ap, wcol=wcol: e.tensor_scalar(
                x1ap, x1ap, wcol, None, op0=ALU.mult), deps=[x2_last] + x1n)
            def pr2(e, rc=rc, nq=nq, wcol=wcol):
                for half in range(2):
                    dstf = bass.AP(art, B_PROC + rc * 2048 + half * 1024,
                                   [[ARW, nq], [1, 1024]]).bitcast(F32)
                    if rc == 0:
                        src = PS5[0:nq, half * 512:(half + 1) * 512]
                    else:
                        src = (PS2 if half == 0 else PS3)[0:nq, :]
                    ins = e.scalar_tensor_tensor(dstf, src, wcol, dstf,
                                                 op0=ALU.mult, op1=ALU.add)
                return ins
            v_p2 = sch.add("v", pr2, deps=[v_p1])
            proc_done.append(v_p2)
        dump(f"procB{b}",
             lambda b=b: bass.AP(art, B_PROC + 2048, [[ARW, 127], [1, 2048]]).bitcast(F32),
             [127, 1024], F32, proc_done)
        dmp_proc = dump(f"proc{b}",
                        lambda b=b: bass.AP(art, B_PROC, [[ARW, 128], [1, 2048]]).bitcast(F32),
                        [128, 1024], F32, proc_done)
        ps5_last = proc_done[-1]

        ld_wout = load_w1024(woutt_h, 8192, gl_done[-2:] + xo_evs)
        out_writes = []
        gated_last = None
        wc_last = proc_done[-1]
        v_f = None
        gated_all = []
        for cc in range(16):
            tglob = b * 16 + cc
            def wcmm(e, cc=cc):
                last = None
                for ec in range(8):
                    lhs = wct_sb[0:16, ec * 128:(ec + 1) * 128]
                    rhs = bass.AP(XMT[:, :].tensor, cc * 128, [[2048, 16], [1, 128]])
                    last = e.matmul(PS5[:, ec * 128:(ec + 1) * 128], lhs, rhs,
                                    start=(ec in (0, 4)), stop=(ec in (3, 7)))
                return last
            t_wc = sch.add("t", wcmm, deps=[v_xmt, ld_wct, wc_last])
            def gat(e, cc=cc):
                off = cc * 128
                xv = bass.AP(art, P_XP + off, [[ARW, 128], [2048, 8], [1, 128]])
                zv = bass.AP(art, P_ZG + off, [[ARW, 128], [2048, 8], [1, 128]])
                g8 = bass.AP(art, C_GATED, [[ARW, 128], [128, 8], [1, 128]])
                ps = bass.AP(PS5[:, :].tensor, 0, [[1024, 128], [128, 8], [1, 128]])
                return CH(e,
                          lambda e: e.tensor_tensor(g8, ps, xv, ALU.add),
                          lambda e: e.tensor_tensor(g8, g8, zv, ALU.mult))
            v_g = sch.add("v", gat, deps=[t_wc, gated_last])
            gated_last = v_g
            gated_all.append(v_g)
            wc_last = v_g
            def wotmm(e):
                last = None
                for half in range(2):
                    bank = PS0 if half == 0 else PS1
                    for fcx in range(8):
                        lhs = bass.AP(art, C_GATED + fcx * 128, [[ARW, 128], [1, 128]])
                        rhs = bass.AP(wart, 8192 + fcx * 1024 + half * 512,
                                      [[16384, 128], [1, 512]])
                        last = e.matmul(bank[:, :], lhs, rhs, start=(fcx == 0), stop=False)
                    last = e.matmul(bank[:, :], cones_rb[:],
                                    bout_rb[0:1, half * 512:(half + 1) * 512],
                                    start=False, stop=True)
                return last
            dep_wo = [v_g, ld_wout, ps01_last[0], ps01_last[1], ld_bout, lc["ones_rb"]]
            t_wo = sch.add("t", wotmm, deps=dep_wo)
            xre = arf(C_XRE[cc % 2], 128, 1024)
            dep_x = [out_writes[-2]] if len(out_writes) >= 2 else \
                (prev_b["c_done"] if b else [])
            d_xre = sch.add("s", lambda e, xre=xre, tglob=tglob: e.dma_start(
                xre, bass.AP(x_h, tglob * 128 * 1024, [[1024, 128], [1, 1024]])),
                deps=dep_x, dma=True)
            outt = arf(C_OUTT[cc % 2], 128, 1024)
            dep_as = [d_xre, v_beta]
            if len(out_writes) >= 2:
                dep_as.append(out_writes[-2])
            elif b:
                dep_as += prev_b["c_done"]
            v_as = sch.add("v", lambda e, outt=outt, xre=xre, tglob=tglob: e.tensor_scalar(
                outt, xre, rk_b1[:, tglob:tglob + 1], rk_b2[:, tglob:tglob + 1],
                op0=ALU.mult, op1=ALU.add), deps=dep_as)
            def fin(e, outt=outt):
                e.tensor_add(bass.AP(art, outt.offset * 2, [[ARW, 128], [1, 1024]]).bitcast(F32),
                             bass.AP(art, outt.offset * 2, [[ARW, 128], [1, 1024]]).bitcast(F32),
                             PS0[:, :])
                e.tensor_add(
                    bass.AP(art, outt.offset * 2 + 1024, [[ARW, 128], [1, 1024]]).bitcast(F32),
                    bass.AP(art, outt.offset * 2 + 1024, [[ARW, 128], [1, 1024]]).bitcast(F32),
                    PS1[:, :])
                return e.tensor_scalar(vdum[:, 0:2], vdum[:, 2:4], 1.0, None, op0=ALU.mult)
            v_f = sch.add("v", fin, deps=[v_as, t_wo])
            ps01_last[0] = v_f
            ps01_last[1] = v_f
            d_out = sch.add("y", lambda e, outt=outt, tglob=tglob: e.dma_start(
                bass.AP(out_h, tglob * 128 * 1024, [[1024, 128], [1, 1024]]), outt),
                deps=[v_f], dma=True, q="yo")
            out_writes.append(d_out)
        dump(f"gated{b}", lambda: arb2(C_GATED, 128, [[128, 8], [1, 128]]),
             [128, 1024], F32, gated_all[-1:])
        dump(f"outt0_{b}", lambda: arf(C_OUTT[0], 128, 1024), [128, 1024], F32,
             [v_f] + out_writes[-2:])
        dump(f"outt1_{b}", lambda: arf(C_OUTT[1], 128, 1024), [128, 1024], F32,
             [v_f] + out_writes[-1:])
        sc2 = []
        for rc in range(2):
            nrow = 128 if rc == 0 else 127
            idx = sp_idx[0:nrow, b * 2 + rc:b * 2 + rc + 1]
            src = bass.AP(art, B_PROC + rc * 2048, [[ARW, nrow], [1, 2048]]).bitcast(F32)
            sc2.append(sch.add("g", lambda e, idx=idx, src=src: e.indirect_dma_start(
                bass.AP(out_h, 0, [[1024, 4096], [1, 1024]]),
                IndirectOffsetOnAxis(idx, 0), src, None, compute_op=ALU.add),
                deps=out_writes + proc_done, dma=True, q="gp"))
        scatter_all += sc2

        ps5_last = gated_all[-1]
        prev_b = dict(
            ps01_free=[v_f],
            ps23_free=[proc_done[-1]],
            rank_wbc_done=[rank_mms[-1], wbc_ev[-1]],
            sumq_last=sumq_s[15],
            sumx_last=sumx_s[15],
            u_last=u_done[15],
            rt_last=rt_mm[15],
            rtmm_tail=[rt_mm[14], rt_mm[15]],
            gated_all=gated_all,
            brev_last=brev_ev[-1],
            bt_done=v_bt,
            c_done=sc2 + [out_writes[-1], v_f],
            b_region_free=[proc_done[-1], v_g],
            war_free=[t_wo],
            proc_free=sc2 + ([dmp_proc] if dmp_proc else []),
        )

    if "dbgst" in debug_names:
        dump("dbgst", lambda: dbg_sb[:, :], [128, 320], F32, dbg_probe_sts)

    with nc.allow_non_contiguous_dma(reason="small strided vector/row loads"):
        with nc.Block() as block:
            sch.emit(block, sems)
    ctx.pop_all()
    return nc, dump_handles


_CACHE = {}


def kernel(**inputs):
    if "nc" not in _CACHE:
        _CACHE["nc"], _CACHE["dumps"] = build_program(debug_names=_DEBUG_NAMES)
    nc = _CACHE["nc"]
    x = np.ascontiguousarray(np.asarray(inputs["L0_attn_x"], np.float32))
    f32 = lambda a: np.ascontiguousarray(np.asarray(a, np.float32))
    i32 = lambda a: np.ascontiguousarray(np.asarray(a, np.int32))
    W_in = f32(inputs["L0_attn_W_in"])
    W_out = f32(inputs["L0_attn_W_out"])
    W_B = f32(inputs["L0_attn_W_B"])
    W_C = f32(inputs["L0_attn_W_C"])
    base = {
        "wint": np.ascontiguousarray(W_in.T),
        "woutt": np.ascontiguousarray(W_out.T),
        "wbt": np.ascontiguousarray(W_B.T),
        "wct": np.ascontiguousarray(W_C.T),
        "wq": f32(inputs["L0_ffn_wq"]),
        "wk": f32(inputs["L0_ffn_wk"]),
        "wv": f32(inputs["L0_ffn_wv"]),
        "wo": f32(inputs["L0_ffn_wo"]),
        "w1": f32(inputs["L0_ffn_w1"]),
        "w2": f32(inputs["L0_ffn_w2"]),
        "rw": f32(inputs["L0_ffn_router_w"]).reshape(D),
        "rb": f32(inputs["L0_ffn_router_b"]).reshape(1),
        "lng": f32(inputs["L0_attn_ln_g"]),
        "lnb": f32(inputs["L0_attn_ln_b"]),
        "binv": f32(inputs["L0_attn_b_in"]),
        "boutv": f32(inputs["L0_attn_b_out"]),
        "dskip": f32(inputs["L0_attn_D_skip"]),
        "logdt": f32(inputs["L0_attn_log_dt"]),
        "alog": f32(inputs["L0_attn_A_log"]),
        "ln1": f32(inputs["L0_ffn_ln1"]),
        "ln2": f32(inputs["L0_ffn_ln2"]),
    }
    pos = i32(inputs["L0_ffn_position_ids"])
    in_maps = []
    for core in range(8):
        m = dict(base)
        m["x"] = np.ascontiguousarray(x[2 * core:2 * core + 2].reshape(NT, D))
        pr = np.stack([pos[min(2 * core + j, 3), :L] for j in range(2)])
        m["posr"] = np.ascontiguousarray(pr)
        in_maps.append(m)
    res = run_bass_kernel_spmd(nc, in_maps, core_ids=list(range(8)))
    _CACHE["last_results"] = res.results
    out = np.concatenate([r["out"].reshape(2, L, D) for r in res.results], axis=0)
    return out


_DEBUG_NAMES = ()
